# version 28
# baseline (speedup 1.0000x reference)
import numpy as np
import ml_dtypes
from contextlib import ExitStack

import concourse.bass as bass
import concourse.mybir as mybir
from concourse.bass_utils import run_bass_kernel_spmd

F32 = mybir.dt.float32
BF16 = mybir.dt.bfloat16
AX = mybir.AxisListType.X
OP = mybir.AluOpType
AF = mybir.ActivationFunctionType

B, S, H = 4, 2048, 2048
NH, KVH, D = 16, 4, 128
FF = 5632
NT_OWN = 1024
NC_F = FF // 128
EPS_LN = 1e-5
EPS_Q = 1e-8
QSCALE = 1.0 / float(np.sqrt(D))
NEG = -1.0e9
G3 = 8


class _Op:
    __slots__ = ("eng", "fn", "waits", "incs")


class Sched:
    def __init__(self):
        self.ops = []
        self.cnt = {}

    def tok_now(self, sem):
        return (sem, self.cnt.get(sem, 0))

    def op(self, eng, fn, after=(), tok=False, dma=None, inc=None):
        waits = [t for t in after if t is not None]
        incs = []
        t = None
        if dma is not None:
            self.cnt[dma] = self.cnt.get(dma, 0) + 16
            t = (dma, self.cnt[dma])
            incs.append((dma, 16))
        elif inc is not None:
            sem, amt = inc
            self.cnt[sem] = self.cnt.get(sem, 0) + amt
            t = (sem, self.cnt[sem])
            incs.append((sem, amt))
        elif tok:
            sem = "e_" + eng
            self.cnt[sem] = self.cnt.get(sem, 0) + 1
            t = (sem, self.cnt[sem])
            incs.append((sem, 1))
        o = _Op()
        o.eng, o.fn, o.waits, o.incs = eng, fn, waits, incs
        self.ops.append(o)
        return t

    def emit(self, eng_name, eng, sems):
        water = {}
        for o in self.ops:
            if o.eng != eng_name:
                continue
            for (s, v) in o.waits:
                if v <= 0 or water.get(s, 0) >= v:
                    continue
                eng.wait_ge(sems[s], v)
                water[s] = v
            ins = o.fn(eng)
            for (s, amt) in o.incs:
                ins.then_inc(sems[s], amt)


class Slot:

    def __init__(self, sched, sem):
        self.S = sched
        self.sem = sem
        self.free = []
        self.ready = None

    def load(self, eng, fns, after=()):
        t = None
        aft = list(self.free) + list(after)
        for fn in fns:
            t = self.S.op(eng, fn, after=aft, dma=self.sem)
            aft = []
        self.free = []
        self.ready = t
        return t

    def consumed(self, tok):
        if tok is not None:
            self.free.append(tok)


def build_nc(debug=False):
    nc = bass.Bass("TRN2", target_bir_lowering=False, debug=debug)

    d_xo = nc.dram_tensor("xo", [NT_OWN, H], F32, kind="ExternalInput").ap()
    d_xa = nc.dram_tensor("xa", [NT_OWN, H], F32, kind="ExternalInput").ap()
    d_wq = nc.dram_tensor("wq", [H, H], F32, kind="ExternalInput").ap()
    d_wk = nc.dram_tensor("wk", [H, KVH * D], F32, kind="ExternalInput").ap()
    d_wv = nc.dram_tensor("wv", [H, KVH * D], F32, kind="ExternalInput").ap()
    d_wo = nc.dram_tensor("wo", [H, H], F32, kind="ExternalInput").ap()
    d_wsl = nc.dram_tensor("wsl", [18, 128, 2048], F32, kind="ExternalInput").ap()
    d_gsl = nc.dram_tensor("gsl", [18, 1], F32, kind="Internal").ap()
    d_gfull = nc.dram_tensor("gfull", [144, 1], F32, kind="Internal").ap()
    d_qsl = nc.dram_tensor("qsl", [18, 128, 2048], BF16, kind="Internal").ap()
    d_qfull = nc.dram_tensor("qfull", [144, 128, 2048], BF16, kind="Internal",
                             addr_space="Shared").ap()
    d_maskw = nc.dram_tensor("maskw", [128, 640], BF16, kind="ExternalInput").ap()
    d_idb = nc.dram_tensor("identb", [128, 128], BF16, kind="ExternalInput").ap()
    d_idf = nc.dram_tensor("identf", [128, 128], F32, kind="ExternalInput").ap()
    d_ng = nc.dram_tensor("ng", [128, 8], F32, kind="ExternalInput").ap()
    d_out = nc.dram_tensor("out", [NT_OWN, H], F32, kind="ExternalOutput").ap()
    d_dbg = {}
    if debug:
        for nm, shape in [("dbg_xl1", [128, 16384]), ("dbg_k", [128, 8192]),
                          ("dbg_v", [128, 8192]), ("dbg_q", [128, 16384]),
                          ("dbg_av", [128, 16384]), ("dbg_xattn", [128, 16384]),
                          ("dbg_xl2", [128, 16384]), ("dbg_tau", [128, 8])]:
            d_dbg[nm] = nc.dram_tensor(nm, shape, F32, kind="ExternalOutput").ap()

    S = Sched()
    sem_names = ["e_tensor", "e_vector", "e_scalar", "e_gpsimd",
                 "f1a", "f1b", "f1c", "wpa", "wpb", "d_const", "d_cast0",
                 "d_cast1", "d_out0", "d_out1", "d_out2", "d_qst", "d_gst",
                 "d_qs0", "d_qs1", "d_qs2",
                 "d_gld", "cc1", "cc2", "qw1l", "qw2l", "wo0", "wo1", "wo2",
                 "d_cast2", "d_cast3"] + \
        [f"q3l{i}" for i in range(8)] + [f"d_dbg{i}" for i in range(8)]

    with ExitStack() as ctx:
        E = ctx.enter_context
        X = E(nc.sbuf_tensor("Xr", [128, 16384], F32))
        A1 = E(nc.sbuf_tensor("A1", [128, 16384], BF16))
        A2 = E(nc.sbuf_tensor("A2", [128, 16384], BF16))
        A3 = E(nc.sbuf_tensor("A3", [128, 8192], BF16))
        A4 = E(nc.sbuf_tensor("A4", [128, 8192], BF16))
        F1 = [E(nc.sbuf_tensor(f"F1{i}", [128, 2048], F32)) for i in range(3)]
        SC = E(nc.sbuf_tensor("SCb", [128, 4736], BF16))
        WQP = E(nc.sbuf_tensor("WQP", [128, 4096], BF16))
        ROW = E(nc.sbuf_tensor("ROWr", [1, 512], F32))
        MW = E(nc.sbuf_tensor("MW", [128, 640], BF16))
        IDB = E(nc.sbuf_tensor("IDB", [128, 128], BF16))
        IDF = E(nc.sbuf_tensor("IDF", [128, 128], F32))
        NGT = E(nc.sbuf_tensor("NGT", [128, 8], F32))
        ST = E(nc.sbuf_tensor("STt", [128, 64], F32))
        NM = E(nc.sbuf_tensor("NMt", [128, 16], F32))
        GA = E(nc.sbuf_tensor("GAt", [128, 144], F32))
        TA = E(nc.sbuf_tensor("TAt", [128, 8], F32))
        G1 = E(nc.sbuf_tensor("G1t", [1, 16], F32))
        G2 = E(nc.sbuf_tensor("G2t", [18, 1], F32))
        GF8 = E(nc.sbuf_tensor("GF8t", [8, 20], F32))
        CNEG = E(nc.sbuf_tensor("CNEG", [128, 1], F32))
        NGC = E(nc.sbuf_tensor("NGC", [128, 8], F32))
        ONESC = E(nc.sbuf_tensor("ONESC", [1, 128], F32))
        ONESPB = E(nc.sbuf_tensor("ONESPB", [128, 1], BF16))
        ONESPF = E(nc.sbuf_tensor("ONESPF", [128, 1], F32))

        PS_A = E(nc.psum_tensor("PS_A", [128, 1536], F32))
        PS_B = E(nc.psum_tensor("PS_B", [128, 1536], F32))
        PS_C = E(nc.psum_tensor("PS_C", [128, 512], F32))
        PS_D = E(nc.psum_tensor("PS_D", [128, 512], F32))

        sems = {n: E(nc.semaphore(n)) for n in sem_names}
        block = E(nc.Block())

        Xv, A1v, A2v, A3v, A4v = X[:], A1[:], A2[:], A3[:], A4[:]
        SCv, WQPv = SC[:], WQP[:]
        X3 = Xv.rearrange("p (c u) -> p c u", c=16)
        A13 = A1v.rearrange("p (c u) -> p c u", c=16)
        A23 = A2v.rearrange("p (c u) -> p c u", c=16)

        f1slots = [Slot(S, n) for n in ("f1a", "f1b", "f1c")]
        wpslots = [Slot(S, n) for n in ("wpa", "wpb")]

        dump_n = [0]

        def dump(name, src, after):
            if not debug:
                return None
            sem = f"d_dbg{dump_n[0]}"
            dump_n[0] += 1
            return S.op("gpsimd", lambda e, d=d_dbg[name], sv=src: e.dma_start(
                out=d, in_=sv), after=after, dma=sem)

        t_const = None
        for (dst, src) in [(MW[:], d_maskw), (IDB[:], d_idb), (IDF[:], d_idf),
                           (NGT[:], d_ng)]:
            t_const = S.op("sync", lambda e, d=dst, s=src: e.dma_start(out=d, in_=s),
                           dma="d_const")
        S.op("vector", lambda e: e.memset(ONESC[:], 1.0))
        S.op("vector", lambda e: e.memset(CNEG[:], -16.0))
        t_ngc = S.op("vector", lambda e: e.tensor_scalar(
            NGC[:], NGT[:], float(np.exp(-16.0)), None, OP.mult),
            after=[t_const], tok=True)
        t_ta0 = S.op("vector", lambda e: e.memset(TA[:], 0.0), tok=True)
        S.op("vector", lambda e: e.memset(ONESPB[:], 1.0))
        t_ones = S.op("vector", lambda e: e.memset(ONESPF[:], 1.0), tok=True)

        gamma_state = {"i": 0, "last": []}

        def emit_gamma(n):
            for _ in range(n):
                j = gamma_state["i"]
                if j >= 18:
                    return
                gamma_state["i"] = j + 1
                sidx = j % 3
                slot = f1slots[sidx]
                tl = slot.load("sync", [
                    (lambda e, sx=sidx, jj=j, hq=hq: e.dma_start(
                        out=F1[sx][:, hq * 1024:(hq + 1) * 1024],
                        in_=d_wsl[jj][:, hq * 1024:(hq + 1) * 1024]))
                    for hq in range(2)])
                if j % 2 == 0:
                    ta = S.op("scalar", lambda e, sx=sidx, c=j: e.activation(
                        F1[sx][:], F1[sx][:], AF.Abs, accum_out=GA[:, c:c + 1]),
                        after=[tl], tok=True)
                else:
                    ta = S.op("vector", lambda e, sx=sidx, c=j: e.reduce_sum(
                        GA[:, c:c + 1], F1[sx][:], AX, apply_absolute_value=True),
                        after=[tl], tok=True)
                slot.consumed(ta)
                gamma_state["last"].append(ta)

        ps_rot = [PS_C, PS_D]
        ps_free = {id(PS_C): None, id(PS_D): None}
        sc_slot_free = [None, None]
        t_xl1_done = None
        t_xl1_half = [None, None]
        t_xfm_done = None
        prev_chain = [None, None]
        for t in range(16):
            own = t >= 8
            slot = f1slots[t % 3]
            sidx = t % 3
            src = d_xo if own else d_xa
            r0 = ((t - 8) if own else t) * 128
            tload = slot.load("sync", [
                (lambda e, s=sidx, ss=src, r=r0, hq=hq: e.dma_start(
                    out=F1[s][:, hq * 1024:(hq + 1) * 1024],
                    in_=ss[r:r + 128, hq * 1024:(hq + 1) * 1024]))
                for hq in range(2)])
            ftile = F1[sidx][:]
            t_red = S.op("vector", lambda e, ft=ftile, c=t: e.reduce_sum(
                ST[:, c:c + 1], ft, AX), after=[tload], tok=True)
            snk = SCv[:, (t % 2) * 2048:(t % 2) * 2048 + 2048]
            nb0 = 6 * (t % 2)
            t_sq = S.op("scalar", lambda e, ft=ftile, c=t, sk=snk: e.activation(
                sk, ft, AF.Square, accum_out=ST[:, 16 + c:17 + c]),
                after=[tload, sc_slot_free[t % 2]], tok=True)
            t_c = S.op("vector", lambda e, c=t: e.tensor_scalar(
                ST[:, 32 + c:33 + c], ST[:, c:c + 1], 1.0 / H, None, OP.mult),
                after=[t_red, prev_chain[t % 2]], tok=True)
            t_c = S.op("vector", lambda e, c=t, n0=nb0: e.tensor_mul(
                NM[:, n0:n0 + 1], ST[:, 32 + c:33 + c], ST[:, 32 + c:33 + c]),
                after=[t_c], tok=True)
            t_c = S.op("vector", lambda e, c=t, n0=nb0: e.tensor_scalar(
                NM[:, n0 + 1:n0 + 2], ST[:, 16 + c:17 + c], 1.0 / H, None, OP.mult),
                after=[t_sq, t_c], tok=True)
            t_c = S.op("vector", lambda e, n0=nb0: e.tensor_sub(
                NM[:, n0 + 2:n0 + 3], NM[:, n0 + 1:n0 + 2], NM[:, n0:n0 + 1]),
                after=[t_c], tok=True)
            t_c = S.op("vector", lambda e, n0=nb0: e.tensor_scalar(
                NM[:, n0 + 2:n0 + 3], NM[:, n0 + 2:n0 + 3], EPS_LN, None, OP.add),
                after=[t_c], tok=True)
            t_c = S.op("vector", lambda e, n0=nb0: e.reciprocal(
                NM[:, n0 + 3:n0 + 4], NM[:, n0 + 2:n0 + 3]), after=[t_c], tok=True)
            t_s = S.op("scalar", lambda e, n0=nb0: e.activation(
                NM[:, n0 + 4:n0 + 5], NM[:, n0 + 3:n0 + 4], AF.Sqrt),
                after=[t_c], tok=True)
            t_c = S.op("vector", lambda e, n0=nb0: e.tensor_mul(
                NM[:, n0 + 5:n0 + 6], NM[:, n0 + 4:n0 + 5], NM[:, n0 + 4:n0 + 5]),
                after=[t_s], tok=True)
            t_c = S.op("vector", lambda e, n0=nb0: e.tensor_mul(
                NM[:, n0 + 5:n0 + 6], NM[:, n0 + 5:n0 + 6], NM[:, n0 + 2:n0 + 3]),
                after=[t_c], tok=True)
            t_c = S.op("vector", lambda e, n0=nb0: e.tensor_scalar(
                NM[:, n0 + 5:n0 + 6], NM[:, n0 + 5:n0 + 6], 3.0, -0.5,
                OP.subtract, OP.mult), after=[t_c], tok=True)
            t_c = S.op("vector", lambda e, c=t, n0=nb0: e.tensor_mul(
                ST[:, 48 + c:49 + c], NM[:, n0 + 4:n0 + 5], NM[:, n0 + 5:n0 + 6]),
                after=[t_c], tok=True)
            prev_chain[t % 2] = t_c
            xt_tm = snk
            t_norm = S.op("vector", lambda e, ft=ftile, c=t, dv=xt_tm: e.tensor_scalar(
                dv, ft, ST[:, 32 + c:33 + c], ST[:, 48 + c:49 + c],
                OP.subtract, OP.mult), after=[t_c, t_sq], tok=True)
            dst3 = A13 if own else A23
            tt_own = (t - 8) if own else t
            for r in range(4):
                ps = ps_rot[r % 2]
                trs = None
                for cc in range(4):
                    dc = 4 * r + cc
                    trs = S.op("tensor", lambda e, p=ps, d=dc, c=cc, sv=xt_tm:
                               e.transpose(
                                   p[:].bitcast(BF16)[:, c * 128:(c + 1) * 128],
                                   sv[:, d * 128:(d + 1) * 128], IDB[:]),
                               after=([t_norm, t_const, ps_free[id(ps)]]
                                      if cc == 0 else []), tok=(cc == 3))
                if r % 2 == 0:
                    ev = S.op("vector", lambda e, p=ps, rr=r, d3=dst3, tt=tt_own:
                              e.tensor_copy(
                                  d3[:, 4 * rr:4 * rr + 4, tt * 128:(tt + 1) * 128],
                                  p[:].bitcast(BF16)[:, 0:512].rearrange(
                                      "p (c u) -> p c u", c=4)),
                              after=[trs], tok=True)
                else:
                    ev = S.op("scalar", lambda e, p=ps, rr=r, d3=dst3, tt=tt_own:
                              e.activation(
                                  d3[:, 4 * rr:4 * rr + 4, tt * 128:(tt + 1) * 128],
                                  p[:].bitcast(BF16)[:, 0:512].rearrange(
                                      "p (c u) -> p c u", c=4), AF.Copy),
                              after=[trs], tok=True)
                ps_free[id(ps)] = ev
                if r % 2 == 0:
                    ev_d_ = ev
                else:
                    ev_a_ = ev
            t_xl1_done = [ev_d_, ev_a_]
            t_xl1_half[1 if own else 0] = t_xl1_done
            sc_slot_free[t % 2] = S.tok_now("e_tensor")
            if own:
                for r in range(4):
                    ps = ps_rot[r % 2]
                    trs = None
                    for cc in range(4):
                        dc = 4 * r + cc
                        trs = S.op("tensor", lambda e, p=ps, d=dc, c=cc, ft=ftile:
                                   e.transpose(p[:, c * 128:(c + 1) * 128],
                                               ft[:, d * 128:(d + 1) * 128], IDF[:]),
                                   after=([tload, t_const, ps_free[id(ps)]]
                                          if cc == 0 else []), tok=(cc == 3))
                    if r % 2 == 0:
                        ev = S.op("vector", lambda e, p=ps, rr=r, tt=tt_own:
                                  e.tensor_copy(
                                      X3[:, 4 * rr:4 * rr + 4,
                                         tt * 128:(tt + 1) * 128],
                                      p[:, 0:512].rearrange("p (c u) -> p c u",
                                                            c=4)),
                                  after=[trs], tok=True)
                    else:
                        ev = S.op("scalar", lambda e, p=ps, rr=r, tt=tt_own:
                                  e.activation(
                                      X3[:, 4 * rr:4 * rr + 4,
                                         tt * 128:(tt + 1) * 128],
                                      p[:, 0:512].rearrange("p (c u) -> p c u",
                                                            c=4), AF.Copy),
                                  after=[trs], tok=True)
                    ps_free[id(ps)] = ev
                    if r % 2 == 0:
                        evx_d_ = ev
                    else:
                        evx_a_ = ev
                t_xfm_done = [evx_d_, evx_a_]
            slot.consumed(t_norm)
            if own:
                slot.consumed(S.tok_now("e_tensor"))
            if t >= 2:
                emit_gamma(2)

        t_dbg_xl1 = dump("dbg_xl1", A1v, list(t_xl1_done))


        def panel_load(slot_i, dw, ocol, after=()):
            slot = wpslots[slot_i]
            dst = WQPv[:, slot_i * 2048:(slot_i + 1) * 2048]
            dst3 = dst.rearrange("p (c o) -> p c o", c=16)
            fns = []
            for cc in range(4):
                sr = dw[cc * 512:(cc + 1) * 512, ocol:ocol + 128].rearrange(
                    "(c p) o -> p c o", p=128)
                fns.append(lambda e, s3=dst3, c4=cc, s=sr: e.dma_start(
                    out=s3[:, c4 * 4:(c4 + 1) * 4, :], in_=s))
            t = slot.load("gpsimd", fns, after=after)
            return slot, dst, t

        t_kv_done = None
        p1_panels = [(wi_, pj_) for wi_ in range(2) for pj_ in range(4)]
        p1_pre = [panel_load(i_ % 2, [d_wk, d_wv][p[0]], p[1] * 128)
                  for i_, p in enumerate(p1_panels[:2])]
        emit_gamma(18)
        ga_last = gamma_state["last"]
        ga_act = [t for t in ga_last if t[0] == "e_scalar"]
        ga_dve = [t for t in ga_last if t[0] == "e_vector"]
        t_glast = ([ga_act[-1]] if ga_act else []) + ([ga_dve[-1]] if ga_dve else [])
        t_gmm = S.op("tensor", lambda e: e.matmul(
            PS_B[0:18, 0:1], GA[:, 0:18], ONESPF[:], start=True, stop=True),
            after=t_glast + [t_ones], tok=True)
        t_gcp = S.op("vector", lambda e: e.tensor_copy(G2[:], PS_B[0:18, 0:1]),
                     after=[t_gmm], tok=True)
        t_gst = S.op("sync", lambda e: e.dma_start(out=d_gsl, in_=G2[:]),
                     after=[t_gcp], dma="d_gst")
        t_cc1 = S.op("gpsimd", lambda e: e.collective_compute(
            "AllGather", OP.bypass, replica_groups=[list(range(8))],
            ins=[d_gsl.opt()], outs=[d_gfull.opt()]),
            after=[t_gst], inc=("cc1", 1))
        t_gld = S.op("sync", lambda e: e.dma_start(
            out=GF8[0:8, 0:18], in_=d_gfull.rearrange("(c j) o -> c (j o)", j=18)),
            after=[t_cc1], dma="d_gld")
        t_tau = []
        prev_tau_tok = None
        for mi in range(3):
            tg = S.op("vector", lambda e, m=mi: e.reduce_sum(
                GF8[0:8, 19:20], GF8[0:8, 6 * m:6 * m + 6],
                AX), after=[t_gld, prev_tau_tok], tok=True)
            tmm = S.op("tensor", lambda e, m=mi: e.matmul(
                PS_B[0:1, 500 + m:501 + m], GF8[0:8, 19:20], ONESPF[0:8, 0:1],
                start=True, stop=True), after=[tg], tok=True)
            tc = S.op("vector", lambda e, m=mi: e.tensor_scalar(
                G1[0:1, m:m + 1], PS_B[0:1, 500 + m:501 + m],
                1.0 / (FF * H), EPS_Q, OP.mult, OP.add), after=[tmm], tok=True)
            tc2 = S.op("vector", lambda e, m=mi: e.tensor_scalar(
                G1[0:1, 4 + m:5 + m], G1[0:1, m:m + 1], 0.5, None, OP.mult),
                after=[tc], tok=True)
            tc3 = S.op("vector", lambda e, m=mi: e.tensor_scalar(
                G1[0:1, 8 + m:9 + m], G1[0:1, m:m + 1], -0.5, None, OP.mult),
                after=[tc2], tok=True)
            tb = S.op("tensor", lambda e, m=mi: e.matmul(
                PS_B[:, 504 + m:505 + m], ONESC[:], G1[0:1, 4 + m:5 + m],
                start=True, stop=True), after=[tc3], tok=True)
            tb2 = S.op("tensor", lambda e, m=mi: e.matmul(
                PS_B[:, 508 + m:509 + m], ONESC[:], G1[0:1, 8 + m:9 + m],
                start=True, stop=True), after=[tb], tok=True)
            tv = S.op("vector", lambda e, m=mi: e.tensor_copy(
                TA[:, m:m + 1], PS_B[:, 504 + m:505 + m]), after=[tb2], tok=True)
            tv = S.op("vector", lambda e, m=mi: e.tensor_copy(
                TA[:, 4 + m:5 + m], PS_B[:, 508 + m:509 + m]), after=[tv], tok=True)
            t_tau.append(tv)
            prev_tau_tok = tv
        t_taud = t_tau[-1]

        q_state = {"i": 0, "prevadd": t_taud,
                   "sthome": [t_taud, t_taud], "pend": None,
                   "stl": [], "ccs": []}

        def _emit_pend():
            pe = q_state["pend"]
            if pe is None:
                return
            q_state["pend"] = None
            jj_, av_, sx_, tadd_ = pe
            tst = S.op("scalar", lambda e, jj=jj_, av=av_: e.dma_start(
                out=d_qsl[jj], in_=av), after=[tadd_], dma=f"d_qs{sx_}")
            q_state["sthome"][jj_ % 2] = tst
            q_state["stl"].append((jj_, tst))
            if jj_ in (5, 11):
                m3 = jj_ // 6
                in_m = d_qsl[6 * m3:6 * m3 + 6].opt()
                out_m = d_qfull[48 * m3:48 * m3 + 48].opt()
                aft_m = [tok for (j2, tok) in q_state["stl"]
                         if 6 * m3 <= j2 < 6 * m3 + 6][-3:]
                q_state["ccs"].append(S.op(
                    "gpsimd", lambda e, i_=in_m, o_=out_m:
                    e.collective_compute(
                        "AllGather", OP.bypass,
                        replica_groups=[list(range(8))],
                        ins=[i_], outs=[o_]),
                    after=aft_m, inc=("cc2", 1)))

        def emit_quant(n):
            for _ in range(n):
                j = q_state["i"]
                if j >= 18:
                    return
                q_state["i"] = j + 1
                m = j // 6
                sidx = j % 3
                slot = f1slots[sidx]
                tl = slot.load("sync", [lambda e, sx=sidx, jj=j: e.dma_start(
                    out=F1[sx][:], in_=d_wsl[jj])])
                _emit_pend()
                h = j % 2
                a_ = SCv[:, h * 2048:h * 2048 + 2048]
                b_ = SCv[:, (1 - h) * 2048:(1 - h) * 2048 + 2048]
                tsa = S.op("scalar", lambda e, sx=sidx, mm_=m, av=a_:
                           e.activation(
                               av, F1[sx][:], AF.Sign,
                               bias=TA[:, 4 + mm_:5 + mm_]),
                           after=[tl, t_tau[m], q_state["prevadd"],
                                  q_state["sthome"][h]], tok=True)
                tsb = S.op("scalar", lambda e, sx=sidx, mm_=m, bv=b_:
                           e.activation(
                               bv, F1[sx][:], AF.Sign, bias=TA[:, mm_:mm_ + 1]),
                           after=[tsa, q_state["sthome"][1 - h]], tok=True)
                slot.consumed(tsb)
                tadd = S.op("vector", lambda e, av=a_, bv=b_: e.tensor_tensor(
                    av, av, bv, OP.add), after=[tsb], tok=True)
                q_state["prevadd"] = tadd
                q_state["pend"] = (j, a_, sidx, tadd)

        for p1i in range(8):
            wi, pj = p1_panels[p1i]
            if True:
                slot, pan, t_pan = p1_pre[p1i]
                t_last_mm = None
                for tc in range(4):
                    src_ar = A2v if tc < 2 else A1v
                    th = tc % 2
                    ps = ps_rot[tc % 2]
                    mm = None
                    for dc in range(16):
                        mm = S.op("tensor", lambda e, p=ps, pn=pan, d=dc, sa=src_ar,
                                  t2=th: e.matmul(
                                      p[:], pn[:, d * 128:(d + 1) * 128],
                                      sa[:, d * 1024 + t2 * 512:
                                         d * 1024 + t2 * 512 + 512],
                                      start=(d == 0), stop=(d == 15)),
                                  after=(([t_pan, ps_free[id(ps)]] +
                                          t_xl1_half[0 if tc < 2 else 1])
                                         if dc == 0 else []), tok=(dc == 15))
                    t_last_mm = mm
                    dst = (A3v if wi == 0 else A4v)[
                        :, pj * 2048 + tc * 512: pj * 2048 + tc * 512 + 512]
                    ev = S.op("vector", lambda e, p=ps, d=dst: e.tensor_copy(
                        d, p[:]), after=[mm], tok=True)
                    ps_free[id(ps)] = ev
                    t_kv_done = ev
                slot.consumed(t_last_mm)
                if p1i + 2 < 8:
                    pn_ = p1_panels[p1i + 2]
                    p1_pre.append(panel_load((p1i + 2) % 2,
                                             [d_wk, d_wv][pn_[0]],
                                             pn_[1] * 128))
                emit_quant(1)
        t_vtm_all = None
        for kvh in range(KVH):
            for kt in range(16):
                base = kvh * 2048 + kt * 128
                ps = ps_rot[kt % 2]
                tr = S.op("tensor", lambda e, p=ps, b=base: e.transpose(
                    p[:].bitcast(BF16)[:, 0:128], A4v[:, b:b + 128], IDB[:]),
                    after=[t_kv_done, ps_free[id(ps)]], tok=True)
                ev = S.op("vector", lambda e, p=ps, b=base: e.tensor_copy(
                    A4v[:, b:b + 128], p[:].bitcast(BF16)[:, 0:128]),
                    after=[tr], tok=True)
                ps_free[id(ps)] = ev
                t_vtm_all = ev

        t_dbg_k = dump("dbg_k", A3v, [t_kv_done])
        t_dbg_v = dump("dbg_v", A4v, [t_vtm_all])

        t_q_done = None
        for pj in range(16):
            slot, pan, t_pan = panel_load(pj % 2, d_wq, pj * 128)
            t_last_mm = None
            for th in range(2):
                ps = ps_rot[th % 2]
                mm = None
                for dc in range(16):
                    mm = S.op("tensor", lambda e, p=ps, pn=pan, d=dc, t2=th:
                              e.matmul(p[:], pn[:, d * 128:(d + 1) * 128],
                                       A1v[:, d * 1024 + t2 * 512:
                                           d * 1024 + t2 * 512 + 512],
                                       start=(d == 0), stop=(d == 15)),
                              after=([t_pan, t_kv_done, ps_free[id(ps)]]
                                     if dc == 0 else []), tok=(dc == 15))
                t_last_mm = mm
                dst = A2v[:, pj * 1024 + th * 512: pj * 1024 + th * 512 + 512]
                ev = S.op("vector", lambda e, p=ps, d=dst: e.tensor_scalar(
                    d, p[:], QSCALE, None, OP.mult), after=[mm], tok=True)
                ps_free[id(ps)] = ev
                t_q_done = ev
            slot.consumed(t_last_mm)
            emit_quant(1)

        t_dbg_q = dump("dbg_q", A2v, [t_q_done])

        emit_quant(18)
        _emit_pend()
        in_m = d_qsl[12:18].opt()
        out_m = d_qfull[96:144].opt()
        aft_m = [tok for (jj2, tok) in q_state["stl"] if 12 <= jj2 < 18][-3:]
        t_cc2 = S.op("gpsimd", lambda e, i_=in_m, o_=out_m:
                     e.collective_compute(
                         "AllGather", OP.bypass,
                         replica_groups=[list(range(8))],
                         ins=[i_], outs=[o_]),
                     after=aft_m, inc=("cc2", 1))
        t_q_sc0, t_q_sc1 = q_state["sthome"]

        sc_rot = [PS_A, PS_B]
        sc_free = {id(PS_A): [t_taud], id(PS_B): [t_taud]}
        avst = {"psfree": ps_free[id(PS_C)], "mm": None, "all": None}
        p_free = [t_q_done, t_q_done]
        nm_free = [None, None]
        pending_av = []
        oproj_jobs = []
        oproj_state = {"load_i": 0, "mm_i": 0, "done": None, "ocd": {}}
        wo_slots = [Slot(S, f"wo{i}") for i in range(3)]
        for sl_ in wo_slots:
            sl_.consumed(t_q_done)
        wov = [WQPv[:, 2560 + i * 512:2560 + (i + 1) * 512] for i in range(3)]
        opps_free = {id(PS_C): None, id(PS_D): None}

        def pump_oproj(n):
            for _ in range(n):
                li = oproj_state["load_i"]
                if li < len(oproj_jobs) and li - oproj_state["mm_i"] < 5:
                    kv_, oc_, th_, _avtok = oproj_jobs[li]
                    if th_ == 0:
                        sl = wo_slots[(li // 2) % 3]
                        src = d_wo[(4 * kv_) * 128:(4 * kv_ + 4) * 128,
                                   oc_ * 128:(oc_ + 1) * 128].rearrange(
                                       "(c p) o -> p c o", p=128)
                        dst3 = wov[(li // 2) % 3].rearrange(
                            "p (c o) -> p c o", c=4)
                        sl.load("gpsimd", [lambda e, d3=dst3, s_=src:
                                           e.dma_start(out=d3, in_=s_)])
                    oproj_state["load_i"] = li + 2 if th_ == 0 else li + 1
                mi = oproj_state["mm_i"]
                if mi >= len(oproj_jobs) or mi >= oproj_state["load_i"]:
                    continue
                kv_, oc_, th_, avtok = oproj_jobs[mi]
                oproj_state["mm_i"] = mi + 1
                sl = wo_slots[(mi // 2) % 3]
                pan = wov[(mi // 2) % 3]
                ps = ps_rot[mi % 2]
                mm = None
                for cc in range(4):
                    mm = S.op("tensor", lambda e, p=ps, pn=pan, c=cc, kv2=kv_,
                              t2=th_: e.matmul(
                                  p[:], pn[:, c * 128:(c + 1) * 128],
                                  A1v[:, (4 * kv2 + c) * 1024 + t2 * 512:
                                      (4 * kv2 + c) * 1024 + t2 * 512 + 512],
                                  start=(c == 0), stop=(c == 3)),
                              after=(([sl.ready, avtok, opps_free[id(ps)],
                                       av_free[id(ps)]] + list(t_xfm_done))
                                     if cc == 0 else []),
                              tok=(cc == 3))
                if th_ == 1:
                    sl.consumed(mm)
                xsl = Xv[:, oc_ * 1024 + th_ * 512:oc_ * 1024 + th_ * 512 + 512]
                ev2 = S.op("vector", lambda e, p=ps, xs=xsl: e.tensor_tensor(
                    xs, p[:], xs, OP.add),
                    after=[mm, oproj_state["xw"]], tok=True)
                opps_free[id(ps)] = ev2
                oproj_state["xw"] = ev2
                oproj_state["done"] = ev2
                if kv_ == 3:
                    oproj_state["ocd"][(oc_, th_)] = ev2
        oproj_state["xw"] = None
        pending_trans = [None]
        pT_toks = []
        pT_prev = []

        av_free = {id(PS_C): ps_free[id(PS_C)],
                   id(PS_D): ps_free[id(PS_D)]}

        def emit_av(kv, q, my_toks):
            psd = [PS_C, PS_D][q % 2]
            mm = None
            for w in range(9):
                kt = q + w
                mm = S.op("tensor", lambda e, p=psd, kv_=kv, k=kt, ww=w:
                          e.matmul(
                              p[:, 0:512],
                              A4v[:, (kv_ * 16 + k) * 128:
                                  (kv_ * 16 + k) * 128 + 128],
                              SCv[:, ww * 512:ww * 512 + 512],
                              start=(ww == 0), stop=(ww == 8)),
                          after=(my_toks + [t_vtm_all, av_free[id(psd)],
                                            opps_free[id(psd)]]
                                 if w == 0 else []),
                          tok=(w == 8))
            ev = S.op("scalar", lambda e, kv_=kv, q_=q, p=psd: e.activation(
                A1v[:, kv_ * 4096:(kv_ + 1) * 4096].rearrange(
                    "p (c u) -> p c u", c=4)[:, :, q_ * 128:(q_ + 1) * 128],
                p[:, 0:512].rearrange("p (c u) -> p c u", c=4), AF.Copy),
                after=[mm, t_dbg_xl1], tok=True)
            av_free[id(psd)] = ev
            avst["psfree"] = ev
            avst["mm"] = mm
            avst["all"] = ev
            if q == 7:
                for oc_ in range(16):
                    for th_ in range(2):
                        oproj_jobs.append((kv, oc_, th_, ev))

        for kvh in range(KVH):
            for qt in range(8):
                for qh2 in range(4):
                    qh = kvh * 4 + qh2
                    par = qh2 % 2
                    nb = 8 * par
                    scp = sc_rot[par]
                    pbuf = WQPv[:, par * 1152:par * 1152 + 1152]
                    kbase = kvh * 2048 + qt * 128
                    qcol = qh * 1024 + qt * 128
                    sf = sc_free[id(scp)]
                    sf = sf if isinstance(sf, list) else [sf]
                    S.op("tensor", lambda e, p=scp: e.matmul(
                        p[:, 0:128], IDB[:], MW[:, 0:128], start=True, stop=False),
                        after=[t_q_done, t_vtm_all, t_const] + sf)
                    S.op("tensor", lambda e, p=scp: e.matmul(
                        p[:, 1024:1152], IDB[:], MW[:, 512:640],
                        start=True, stop=False))
                    S.op("tensor", lambda e, p=scp, q=qcol, k=kbase: e.matmul(
                        p[:, 0:128], A2v[:, q:q + 128], A3v[:, k:k + 128],
                        start=False, stop=True))
                    S.op("tensor", lambda e, p=scp, q=qcol, k=kbase: e.matmul(
                        p[:, 128:512], A2v[:, q:q + 128], A3v[:, k + 128:k + 512],
                        start=True, stop=True))
                    S.op("tensor", lambda e, p=scp, q=qcol, k=kbase: e.matmul(
                        p[:, 512:1024], A2v[:, q:q + 128], A3v[:, k + 512:k + 1024],
                        start=True, stop=True))
                    t_sc = S.op("tensor", lambda e, p=scp, q=qcol, k=kbase: e.matmul(
                        p[:, 1024:1152], A2v[:, q:q + 128], A3v[:, k + 1024:k + 1152],
                        start=False, stop=True), tok=True)
                    t_exp = S.op("scalar", lambda e, p=scp, pb=pbuf, n=nb:
                                 e.activation(pb, p[:, 0:1152], AF.Exp,
                                              bias=CNEG[:],
                                              accum_out=NM[:, n:n + 1]),
                                 after=[t_sc, p_free[par], nm_free[par]], tok=True)
                    if pending_trans[0] is not None:
                        pT_prev.append(pending_trans[0]())
                        pending_trans[0] = None
                    if qh2 == 0:
                        for fn_ in pending_av:
                            fn_()
                        pending_av = []
                        pT_toks = pT_prev
                        pT_prev = []
                    t_z = S.op("vector", lambda e, q=qt, n=nb: e.tensor_tensor(
                        NM[:, n + 1:n + 2], NM[:, n:n + 1], NGC[:, q:q + 1],
                        OP.subtract), after=[t_exp, t_ngc], tok=True)
                    t_z = S.op("vector", lambda e, n=nb: e.reciprocal(
                        NM[:, n + 2:n + 3], NM[:, n + 1:n + 2]), after=[t_z],
                        tok=True)
                    t_nrm = S.op("vector", lambda e, pb=pbuf, n=nb: e.tensor_scalar(
                        pb, pb, NM[:, n + 2:n + 3], None, OP.mult),
                        after=[t_z], tok=True)
                    nm_free[par] = t_nrm

                    def mk_trans(scp_=scp, pbuf_=pbuf, par_=par, qh2_=qh2,
                                 t_nrm_=t_nrm, t_exp_=t_exp, kvh_=kvh, qt_=qt):
                        def emit_trans():
                            trs = None
                            for w in range(9):
                                trs = S.op("tensor", lambda e, p=scp_, pb=pbuf_,
                                           ww=w: e.transpose(
                                               p[:].bitcast(BF16)
                                               [:, ww * 128:(ww + 1) * 128],
                                               pb[:, ww * 128:(ww + 1) * 128],
                                               IDB[:]),
                                           after=([t_nrm_, t_exp_] if w == 0
                                                  else []), tok=(w == 8))
                            p3g = ([t_q_sc0, t_q_sc1]
                                   if (kvh_ == 0 and qt_ == 0 and qh2_ == 0)
                                   else [])
                            ev_b = S.op("vector", lambda e, p=scp_, q2=qh2_:
                                        e.tensor_copy(
                                            SCv[:, 0:4608].rearrange(
                                                "p (w u) -> p w u", w=9)
                                            [:, :, q2 * 128:(q2 + 1) * 128],
                                            p[:].bitcast(BF16)[:, 0:1152]
                                            .rearrange("p (w u) -> p w u",
                                                       w=9)),
                                        after=[trs, avst["mm"]] + p3g, tok=True)
                            sc_free[id(scp_)] = [ev_b]
                            p_free[par_] = trs
                            return ev_b
                        return emit_trans

                    pending_trans[0] = mk_trans()
                    if qh2 >= 2:
                        pump_oproj(2)
                pending_av.append(
                    (lambda kv=kvh, q=qt: emit_av(kv, q, list(pT_toks) +
                                                  list(pT_prev) +
                                                  [S.tok_now("e_vector")])))
        if pending_trans[0] is not None:
            pT_prev.append(pending_trans[0]())
            pending_trans[0] = None
        pT_toks = pT_toks + pT_prev
        for fn_ in pending_av:
            fn_()
        pending_av = []
        t_av_all = avst["all"]
        av_mm_prev = avst["mm"]
        for i_ in range(2):
            wpslots[i_].consumed(av_mm_prev)

        ocd = oproj_state["ocd"]
        cast_free = {(h_, c_): t_av_all for h_ in range(2) for c_ in range(2)}
        mmx_h = [None, None]
        mmq_h = [None, None]
        sc_a = sc_free[id(PS_A)]
        sc_b = sc_free[id(PS_B)]
        sc_a = sc_a if isinstance(sc_a, list) else [sc_a]
        sc_b = sc_b if isinstance(sc_b, list) else [sc_b]
        for dc in range(16):
            pump_oproj(2)
            for hh2 in range(2):
                cslot = dc % 2
                xb = SCv[:, hh2 * 1024 + cslot * 512:
                         hh2 * 1024 + cslot * 512 + 512]
                xb2 = SCv[:, 2048 + hh2 * 1024 + cslot * 512:
                          2048 + hh2 * 1024 + cslot * 512 + 512]
                tcast = S.op("gpsimd", lambda e, d=dc, h2=hh2, b=xb: e.dma_start(
                    out=b,
                    in_=Xv[:, d * 1024 + h2 * 512:d * 1024 + h2 * 512 + 512]),
                    after=[ocd[(dc, hh2)], cast_free[(hh2, cslot)]],
                    dma=f"d_cast{2 * hh2 + cslot}")
                tsq = S.op("scalar", lambda e, b=xb, b2=xb2: e.activation(
                    b2, b, AF.Square),
                    after=[tcast, cast_free[(hh2, cslot)]], tok=True)
                mmx_h[hh2] = S.op("tensor", lambda e, b=xb, d=dc, h2=hh2:
                                  e.matmul(
                                      PS_A[0:1, h2 * 512:h2 * 512 + 512],
                                      ONESPB[:], b,
                                      start=(d == 0), stop=(d == 15)),
                                  after=([tcast, t_ones] +
                                         (sc_a if dc == 0 else [])),
                                  tok=True)
                mmq_h[hh2] = S.op("tensor", lambda e, b2=xb2, d=dc, h2=hh2:
                                  e.matmul(
                                      PS_B[0:1, h2 * 512:h2 * 512 + 512],
                                      ONESPB[:], b2,
                                      start=(d == 0), stop=(d == 15)),
                                  after=([tsq] + (sc_b if dc == 0 else [])),
                                  tok=True)
                cast_free[(hh2, cslot)] = mmq_h[hh2]
        pump_oproj(8)
        t_oproj_done = oproj_state["done"]
        x_write_tok = oproj_state["xw"]
        ps_free[id(PS_C)] = opps_free[id(PS_C)]
        ps_free[id(PS_D)] = opps_free[id(PS_D)]

        t_dbg_av = dump("dbg_av", A1v, [t_av_all])
        t_dbg_xattn = dump("dbg_xattn", Xv, [t_oproj_done])

        t_xl2_done = None
        stats_f1_guard = list(f1slots[1].free)
        f1slots[1].free = []
        for hh in range(2):
            hs = hh * 512
            m_sb = F1[1][0:1, 0:512]
            q2_sb = F1[1][0:1, 512:1024]
            t0_sb = F1[1][0:1, 1024:1536]
            a_sb = F1[1][0:1, 1536:2048]
            u_sb = F1[2][0:1, 512:1024]
            r0_sb = F1[2][0:1, 1024:1536]
            tc = S.op("vector", lambda e, h2=hh: e.tensor_scalar(
                m_sb, PS_A[0:1, h2 * 512:h2 * 512 + 512], 1.0 / H, None,
                OP.mult),
                after=[mmx_h[hh], mmq_h[hh], t_xl2_done] + stats_f1_guard,
                tok=True)
            tc = S.op("vector", lambda e, h2=hh: e.tensor_scalar(
                q2_sb, PS_B[0:1, h2 * 512:h2 * 512 + 512], 1.0 / H, None,
                OP.mult), after=[tc], tok=True)
            tc = S.op("vector", lambda e: e.tensor_mul(
                t0_sb, m_sb, m_sb), after=[tc], tok=True)
            tc = S.op("vector", lambda e: e.tensor_sub(
                a_sb, q2_sb, t0_sb), after=[tc], tok=True)
            tc = S.op("vector", lambda e: e.tensor_scalar(
                a_sb, a_sb, EPS_LN, None, OP.add), after=[tc], tok=True)
            tc = S.op("vector", lambda e: e.reciprocal(
                u_sb, a_sb), after=[tc], tok=True)
            ts_ = S.op("scalar", lambda e: e.activation(
                r0_sb, u_sb, AF.Sqrt), after=[tc], tok=True)
            tc = S.op("vector", lambda e: e.tensor_mul(
                t0_sb, r0_sb, r0_sb), after=[ts_], tok=True)
            tc = S.op("vector", lambda e: e.tensor_mul(
                t0_sb, t0_sb, a_sb), after=[tc], tok=True)
            tc = S.op("vector", lambda e: e.tensor_scalar(
                t0_sb, t0_sb, 3.0, -0.5, OP.subtract, OP.mult),
                after=[tc], tok=True)
            tA = S.op("vector", lambda e: e.tensor_mul(
                ROW[0:1, 0:512], r0_sb, t0_sb), after=[tc], tok=True)
            tbA = S.op("tensor", lambda e, h2=hh: e.matmul(
                PS_B[:, h2 * 512:h2 * 512 + 512], ONESC[:], ROW[0:1, 0:512],
                start=True, stop=True), after=[tA], tok=True)
            tB = S.op("vector", lambda e: e.tensor_mul(
                t0_sb, m_sb, ROW[0:1, 0:512]), after=[tbA], tok=True)
            tB = S.op("vector", lambda e: e.tensor_scalar(
                ROW[0:1, 0:512], t0_sb, -1.0, None, OP.mult),
                after=[tB], tok=True)
            tbB = S.op("tensor", lambda e, h2=hh: e.matmul(
                PS_A[:, h2 * 512:h2 * 512 + 512], ONESC[:], ROW[0:1, 0:512],
                start=True, stop=True), after=[tB], tok=True)
            stats_f1_guard = []
            t_n = None
            for dc in range(16):
                tmp = F1[2][:]
                t1 = S.op("vector", lambda e, d=dc, h2=hh, tp=tmp:
                          e.tensor_tensor(
                              tp[:, 0:512],
                              Xv[:, d * 1024 + h2 * 512:
                                 d * 1024 + h2 * 512 + 512],
                              PS_B[:, h2 * 512:h2 * 512 + 512], OP.mult),
                          after=([tbB, t_n] +
                                 (list(f1slots[2].free) if dc == 0 else [])),
                          tok=True)
                t_n = S.op("vector", lambda e, d=dc, h2=hh, tp=tmp:
                           e.tensor_tensor(
                               A2v[:, d * 1024 + h2 * 512:
                                   d * 1024 + h2 * 512 + 512],
                               tp[:, 0:512],
                               PS_A[:, h2 * 512:h2 * 512 + 512], OP.add),
                           after=[t1, t_dbg_q], tok=True)
            if hh == 0:
                f1slots[2].free = []
            t_xl2_done = t_n
        f1slots[2].consumed(t_xl2_done)
        f1slots[1].consumed(t_xl2_done)

        t_dbg_xl2 = dump("dbg_xl2", A2v, [t_xl2_done])
        t_dbg_tau = dump("dbg_tau", TA[:], t_tau)

        x1_free = t_xl2_done
        x2_free = t_xl2_done
        qw3_free = [[t_av_all, t_dbg_k, t_dbg_v] for _ in range(G3)]
        hid_free = [[t_oproj_done, t_dbg_av] for _ in range(G3)]
        qw12_free = [[t_oproj_done] + list(wpslots[0].free),
                     [t_oproj_done] + list(wpslots[1].free)]
        sg_free = None
        x3ps_free = {id(PS_C): ps_free[id(PS_C)], id(PS_D): ps_free[id(PS_D)]}
        hid_ready_g = []
        q3_toks = []
        t_cc2tok = t_cc2

        def gidx(mat, i):
            return 48 * mat + i

        for i in range(NC_F):
            g, ig = i // G3, i % G3
            qw1 = WQPv[:, 0:2048]
            qw2 = WQPv[:, 2048:4096]
            t_l1 = S.op("sync", lambda e, ii=i: e.dma_start(
                out=qw1, in_=d_qfull[gidx(0, ii)]),
                after=qw12_free[0] + [t_cc2tok], dma="qw1l")
            t_l2 = S.op("sync", lambda e, ii=i: e.dma_start(
                out=qw2, in_=d_qfull[gidx(1, ii)]),
                after=qw12_free[1] + [t_cc2tok], dma="qw2l")
            if ig < 4:
                qw3 = A3v[:, ig * 2048:(ig + 1) * 2048]
            else:
                qw3 = A4v[:, (ig - 4) * 2048:(ig - 4 + 1) * 2048]
            t_l3 = S.op("sync", lambda e, ii=i, q3=qw3: e.dma_start(
                out=q3, in_=d_qfull[gidx(2, ii)]),
                after=list(qw3_free[ig]) + [t_cc2tok], dma=f"q3l{ig}")
            q3_toks.append(t_l3)

            mm1 = None
            for th in range(2):
                for dc in range(16):
                    mm1 = S.op("tensor", lambda e, d=dc, t2=th: e.matmul(
                        PS_A[:, t2 * 512:t2 * 512 + 512],
                        qw1[:, d * 128:(d + 1) * 128],
                        A2v[:, d * 1024 + t2 * 512:d * 1024 + t2 * 512 + 512],
                        start=(d == 0), stop=(d == 15)),
                        after=([t_l1, t_xl2_done, x1_free] if (th == 0 and dc == 0)
                               else []), tok=(th == 1 and dc == 15))
            qw12_free[0] = [mm1]
            mm2 = None
            for th in range(2):
                for dc in range(16):
                    mm2 = S.op("tensor", lambda e, d=dc, t2=th: e.matmul(
                        PS_B[:, t2 * 512:t2 * 512 + 512],
                        qw2[:, d * 128:(d + 1) * 128],
                        A2v[:, d * 1024 + t2 * 512:d * 1024 + t2 * 512 + 512],
                        start=(d == 0), stop=(d == 15)),
                        after=([t_l2, x2_free] if (th == 0 and dc == 0) else []),
                        tok=(th == 1 and dc == 15))
            qw12_free[1] = [mm2]

            hidv = A1v[:, ig * 1024:(ig + 1) * 1024]
            t_hid = None
            tsg = None
            for th in range(2):
                sg = SCv[:, 4096:4608]
                tsg = S.op("scalar", lambda e, t2=th, s2=sg: e.activation(
                    s2, PS_A[:, t2 * 512:t2 * 512 + 512], AF.Sigmoid,
                    scale=0.5), after=[mm1, sg_free], tok=True)
                th1 = S.op("vector", lambda e, t2=th, h=hidv: e.tensor_scalar(
                    h[:, t2 * 512:t2 * 512 + 512],
                    PS_B[:, t2 * 512:t2 * 512 + 512], 0.25, None, OP.mult),
                    after=[mm2] + list(hid_free[ig] or []), tok=True)
                t_hid = S.op("vector", lambda e, t2=th, h=hidv, s2=sg:
                             e.tensor_tensor(h[:, t2 * 512:t2 * 512 + 512],
                                             h[:, t2 * 512:t2 * 512 + 512],
                                             s2, OP.mult),
                             after=[tsg, th1], tok=True)
                sg_free = t_hid
            x1_free = tsg
            x2_free = t_hid
            hid_free[ig] = []
            hid_ready_g.append(t_hid)

            if (ig == G3 - 1) or (i == NC_F - 1):
                n_in_g = ig + 1
                hid_ready = hid_ready_g[-1]
                ld_toks = q3_toks[-n_in_g:]
                t_grp = None
                for dc in range(16):
                    for th in range(2):
                        ps = ps_rot[(dc * 2 + th) % 2]
                        mm = None
                        for j in range(n_in_g):
                            if j < 4:
                                q3 = A3v[:, j * 2048 + dc * 128:
                                         j * 2048 + dc * 128 + 128]
                            else:
                                q3 = A4v[:, (j - 4) * 2048 + dc * 128:
                                         (j - 4) * 2048 + dc * 128 + 128]
                            hj = A1v[:, j * 1024 + th * 512:
                                     j * 1024 + th * 512 + 512]
                            mm = S.op("tensor", lambda e, p=ps, qq=q3, hh_=hj,
                                      j2=j, ng=n_in_g: e.matmul(
                                          p[:], qq, hh_, start=(j2 == 0),
                                          stop=(j2 == ng - 1)),
                                      after=([hid_ready, x3ps_free[id(ps)]]
                                             + ld_toks if j == 0
                                             else []), tok=(j == n_in_g - 1))
                        xsl = Xv[:, dc * 1024 + th * 512:dc * 1024 + th * 512 + 512]
                        ev = S.op("vector", lambda e, p=ps, xs=xsl: e.tensor_tensor(
                            xs, p[:], xs, OP.add),
                            after=[mm, x_write_tok, t_dbg_xattn], tok=True)
                        x3ps_free[id(ps)] = ev
                        t_grp = ev
                x_write_tok = t_grp
                for k in range(G3):
                    qw3_free[k] = [t_grp]
                    hid_free[k] = [t_grp]
                hid_ready_g = []

        t_ffn_done = x_write_tok

        t_out_sem_total = None
        for tt in range(8):
            sidx = tt % 3
            slot = f1slots[sidx]
            evs = []
            slot_guard = list(slot.free)
            slot.free = []
            for r in range(4):
                ps = ps_rot[r % 2]
                trs = None
                for cc in range(4):
                    dc = 4 * r + cc
                    trs = S.op("tensor", lambda e, p=ps, d=dc, c=cc, t2=tt:
                               e.transpose(p[:, c * 128:(c + 1) * 128],
                                           Xv[:, d * 1024 + t2 * 128:
                                              d * 1024 + t2 * 128 + 128], IDF[:]),
                               after=([t_ffn_done, ps_free[id(ps)],
                                       x3ps_free[id(ps)]] + slot_guard
                                      if cc == 0 else []), tok=(cc == 3))
                ev = S.op("vector", lambda e, p=ps, rr=r, s=sidx: e.tensor_copy(
                    F1[s][:, rr * 512:(rr + 1) * 512], p[:, 0:512]),
                    after=[trs] + (slot_guard if r == 0 else []), tok=True)
                ps_free[id(ps)] = ev
                x3ps_free[id(ps)] = ev
                evs.append(ev)
            t_st = S.op("sync", lambda e, s=sidx, t2=tt: e.dma_start(
                out=d_out[t2 * 128:(t2 + 1) * 128, :], in_=F1[s][:]),
                after=evs, dma=f"d_out{sidx}")
            slot.consumed(t_st)

        total_out = [S.cnt.get(f"d_out{i}", 0) for i in range(3)]

        @block.sync
        def _(sync):
            S.emit("sync", sync, sems)

        @block.gpsimd
        def _(gpsimd):
            S.emit("gpsimd", gpsimd, sems)
            for i_ in range(3):
                gpsimd.wait_ge(sems[f"d_out{i_}"], total_out[i_])
            for i_ in range(dump_n[0]):
                gpsimd.wait_ge(sems[f"d_dbg{i_}"], 16)

        @block.tensor
        def _(tensor):
            S.emit("tensor", tensor, sems)

        @block.vector
        def _(vector):
            S.emit("vector", vector, sems)

        @block.scalar
        def _(scalar):
            S.emit("scalar", scalar, sems)

    return nc


def _host_consts():
    ii = np.arange(128)[:, None]
    maskw = np.zeros((128, 640), np.float32)
    uu = np.arange(128)[None, :]
    maskw[:, 0:128] = np.where(uu >= ii, 0.0, NEG)
    maskw[:, 512:640] = np.where(uu <= ii, 0.0, NEG)
    maskw = maskw.astype(ml_dtypes.bfloat16)
    identb = np.eye(128, dtype=ml_dtypes.bfloat16)
    identf = np.eye(128, dtype=np.float32)
    return maskw, identb, identf


def _ng_for(h):
    ng = np.zeros((128, 8), np.float32)
    if h == 0:
        ii = np.arange(128)[:, None]
        for qt in range(8):
            ng[:, qt:qt + 1] = 128.0 * (8 - qt) - ii
    return ng


_NC_CACHE = {}


def _get_nc(debug=False):
    if debug not in _NC_CACHE:
        _NC_CACHE[debug] = build_nc(debug=debug)
    return _NC_CACHE[debug]


def make_in_maps(inputs):
    x = np.asarray(inputs["L0_attn_hidden_states"], np.float32)
    maskw, identb, identf = _host_consts()
    zeros = np.zeros((NT_OWN, H), np.float32)
    common = {
        "wq": np.asarray(inputs["L0_attn_Wq"], np.float32),
        "wk": np.asarray(inputs["L0_attn_Wk"], np.float32),
        "wv": np.asarray(inputs["L0_attn_Wv"], np.float32),
        "wo": np.asarray(inputs["L0_attn_Wo"], np.float32),
        "maskw": maskw, "identb": identb, "identf": identf,
    }
    w1 = np.asarray(inputs["L0_ffn_w1"], np.float32)
    w2 = np.asarray(inputs["L0_ffn_w2"], np.float32)
    w3 = np.asarray(inputs["L0_ffn_w3"], np.float32)
    zunit = np.zeros((128, 2048), np.float32)

    def unit(mat, idx):
        if idx >= 44:
            return zunit
        if mat == 0:
            return w1[idx * 128:(idx + 1) * 128, :]
        if mat == 1:
            return w2[idx * 128:(idx + 1) * 128, :]
        return np.ascontiguousarray(
            w3[:, idx * 128:(idx + 1) * 128].reshape(16, 128, 128)
            .transpose(1, 0, 2).reshape(128, 2048))

    def unit_t(mat, idx):
        u = unit(mat, idx)
        return np.ascontiguousarray(
            u.reshape(128, 16, 128).transpose(2, 1, 0).reshape(128, 2048))

    in_maps = []
    for c in range(8):
        b, h = c >> 1, c & 1
        m = dict(common)
        m["xo"] = np.ascontiguousarray(x[b, h * 1024:(h + 1) * 1024, :])
        m["xa"] = (np.ascontiguousarray(x[b, 0:1024, :]) if h == 1 else zeros)
        m["ng"] = _ng_for(h)
        m["wsl"] = np.stack([unit_t(j // 6, 6 * c + (j % 6))
                             for j in range(18)])
        in_maps.append(m)
    return in_maps


def kernel(**inputs):
    nc = _get_nc(debug=False)
    in_maps = make_in_maps(inputs)
    res = run_bass_kernel_spmd(nc, in_maps, core_ids=list(range(8)))
    out = np.empty((B, S, H), np.float32)
    for c in range(8):
        b, h = c >> 1, c & 1
        out[b, h * 1024:(h + 1) * 1024, :] = res.results[c]["out"]
    return out


# version 29
# speedup vs baseline: 1.0197x; 1.0197x over previous
import numpy as np
import ml_dtypes
from contextlib import ExitStack

import concourse.bass as bass
import concourse.mybir as mybir
from concourse.bass_utils import run_bass_kernel_spmd

F32 = mybir.dt.float32
BF16 = mybir.dt.bfloat16
AX = mybir.AxisListType.X
OP = mybir.AluOpType
AF = mybir.ActivationFunctionType

B, S, H = 4, 2048, 2048
NH, KVH, D = 16, 4, 128
FF = 5632
NT_OWN = 1024
NC_F = FF // 128
EPS_LN = 1e-5
EPS_Q = 1e-8
QSCALE = 1.0 / float(np.sqrt(D))
NEG = -1.0e9
G3 = 8


class _Op:
    __slots__ = ("eng", "fn", "waits", "incs")


class Sched:
    def __init__(self):
        self.ops = []
        self.cnt = {}

    def tok_now(self, sem):
        return (sem, self.cnt.get(sem, 0))

    def op(self, eng, fn, after=(), tok=False, dma=None, inc=None):
        waits = [t for t in after if t is not None]
        incs = []
        t = None
        if dma is not None:
            self.cnt[dma] = self.cnt.get(dma, 0) + 16
            t = (dma, self.cnt[dma])
            incs.append((dma, 16))
        elif inc is not None:
            sem, amt = inc
            self.cnt[sem] = self.cnt.get(sem, 0) + amt
            t = (sem, self.cnt[sem])
            incs.append((sem, amt))
        elif tok:
            sem = "e_" + eng
            self.cnt[sem] = self.cnt.get(sem, 0) + 1
            t = (sem, self.cnt[sem])
            incs.append((sem, 1))
        o = _Op()
        o.eng, o.fn, o.waits, o.incs = eng, fn, waits, incs
        self.ops.append(o)
        return t

    def emit(self, eng_name, eng, sems):
        water = {}
        for o in self.ops:
            if o.eng != eng_name:
                continue
            for (s, v) in o.waits:
                if v <= 0 or water.get(s, 0) >= v:
                    continue
                eng.wait_ge(sems[s], v)
                water[s] = v
            ins = o.fn(eng)
            for (s, amt) in o.incs:
                ins.then_inc(sems[s], amt)


class Slot:

    def __init__(self, sched, sem):
        self.S = sched
        self.sem = sem
        self.free = []
        self.ready = None

    def load(self, eng, fns, after=()):
        t = None
        aft = list(self.free) + list(after)
        for fn in fns:
            t = self.S.op(eng, fn, after=aft, dma=self.sem)
            aft = []
        self.free = []
        self.ready = t
        return t

    def consumed(self, tok):
        if tok is not None:
            self.free.append(tok)


def build_nc(debug=False):
    nc = bass.Bass("TRN2", target_bir_lowering=False, debug=debug)

    d_xo = nc.dram_tensor("xo", [NT_OWN, H], F32, kind="ExternalInput").ap()
    d_xa = nc.dram_tensor("xa", [NT_OWN, H], F32, kind="ExternalInput").ap()
    d_wq = nc.dram_tensor("wq", [H, H], F32, kind="ExternalInput").ap()
    d_wk = nc.dram_tensor("wk", [H, KVH * D], F32, kind="ExternalInput").ap()
    d_wv = nc.dram_tensor("wv", [H, KVH * D], F32, kind="ExternalInput").ap()
    d_wo = nc.dram_tensor("wo", [H, H], F32, kind="ExternalInput").ap()
    d_wsl = nc.dram_tensor("wsl", [18, 128, 2048], F32, kind="ExternalInput").ap()
    d_gsl = nc.dram_tensor("gsl", [18, 1], F32, kind="Internal").ap()
    d_gfull = nc.dram_tensor("gfull", [144, 1], F32, kind="Internal").ap()
    d_qsl = nc.dram_tensor("qsl", [18, 128, 2048], BF16, kind="Internal").ap()
    d_qfull = nc.dram_tensor("qfull", [144, 128, 2048], BF16, kind="Internal",
                             addr_space="Shared").ap()
    d_maskw = nc.dram_tensor("maskw", [128, 640], BF16, kind="ExternalInput").ap()
    d_idb = nc.dram_tensor("identb", [128, 128], BF16, kind="ExternalInput").ap()
    d_idf = nc.dram_tensor("identf", [128, 128], F32, kind="ExternalInput").ap()
    d_ng = nc.dram_tensor("ng", [128, 8], F32, kind="ExternalInput").ap()
    d_out = nc.dram_tensor("out", [NT_OWN, H], F32, kind="ExternalOutput").ap()
    d_dbg = {}
    if debug:
        for nm, shape in [("dbg_xl1", [128, 16384]), ("dbg_k", [128, 8192]),
                          ("dbg_v", [128, 8192]), ("dbg_q", [128, 16384]),
                          ("dbg_av", [128, 16384]), ("dbg_xattn", [128, 16384]),
                          ("dbg_xl2", [128, 16384]), ("dbg_tau", [128, 8])]:
            d_dbg[nm] = nc.dram_tensor(nm, shape, F32, kind="ExternalOutput").ap()

    S = Sched()
    sem_names = ["e_tensor", "e_vector", "e_scalar", "e_gpsimd",
                 "f1a", "f1b", "f1c", "wpa", "wpb", "d_const", "d_cast0",
                 "d_cast1", "d_out0", "d_out1", "d_out2", "d_qst", "d_gst",
                 "f1ax", "f1bx", "f1cx",
                 "d_qs0", "d_qs1", "d_qs2",
                 "d_gld", "cc1", "cc2", "qw1l", "qw2l", "wo0", "wo1", "wo2",
                 "d_cast2", "d_cast3"] + \
        [f"q3l{i}" for i in range(8)] + [f"d_dbg{i}" for i in range(8)]

    with ExitStack() as ctx:
        E = ctx.enter_context
        X = E(nc.sbuf_tensor("Xr", [128, 16384], F32))
        A1 = E(nc.sbuf_tensor("A1", [128, 16384], BF16))
        A2 = E(nc.sbuf_tensor("A2", [128, 16384], BF16))
        A3 = E(nc.sbuf_tensor("A3", [128, 8192], BF16))
        A4 = E(nc.sbuf_tensor("A4", [128, 8192], BF16))
        F1 = [E(nc.sbuf_tensor(f"F1{i}", [128, 2048], F32)) for i in range(3)]
        SC = E(nc.sbuf_tensor("SCb", [128, 4736], BF16))
        WQP = E(nc.sbuf_tensor("WQP", [128, 4096], BF16))
        ROW = E(nc.sbuf_tensor("ROWr", [1, 512], F32))
        MW = E(nc.sbuf_tensor("MW", [128, 640], BF16))
        IDB = E(nc.sbuf_tensor("IDB", [128, 128], BF16))
        IDF = E(nc.sbuf_tensor("IDF", [128, 128], F32))
        NGT = E(nc.sbuf_tensor("NGT", [128, 8], F32))
        ST = E(nc.sbuf_tensor("STt", [128, 64], F32))
        NM = E(nc.sbuf_tensor("NMt", [128, 16], F32))
        GA = E(nc.sbuf_tensor("GAt", [128, 144], F32))
        TA = E(nc.sbuf_tensor("TAt", [128, 8], F32))
        G1 = E(nc.sbuf_tensor("G1t", [1, 16], F32))
        G2 = E(nc.sbuf_tensor("G2t", [18, 1], F32))
        GF8 = E(nc.sbuf_tensor("GF8t", [8, 20], F32))
        CNEG = E(nc.sbuf_tensor("CNEG", [128, 1], F32))
        NGC = E(nc.sbuf_tensor("NGC", [128, 8], F32))
        ONESC = E(nc.sbuf_tensor("ONESC", [1, 128], F32))
        ONESPB = E(nc.sbuf_tensor("ONESPB", [128, 1], BF16))
        ONESPF = E(nc.sbuf_tensor("ONESPF", [128, 1], F32))

        PS_A = E(nc.psum_tensor("PS_A", [128, 1536], F32))
        PS_B = E(nc.psum_tensor("PS_B", [128, 1536], F32))
        PS_C = E(nc.psum_tensor("PS_C", [128, 512], F32))
        PS_D = E(nc.psum_tensor("PS_D", [128, 512], F32))

        sems = {n: E(nc.semaphore(n)) for n in sem_names}
        block = E(nc.Block())

        Xv, A1v, A2v, A3v, A4v = X[:], A1[:], A2[:], A3[:], A4[:]
        SCv, WQPv = SC[:], WQP[:]
        X3 = Xv.rearrange("p (c u) -> p c u", c=16)
        A13 = A1v.rearrange("p (c u) -> p c u", c=16)
        A23 = A2v.rearrange("p (c u) -> p c u", c=16)

        f1slots = [Slot(S, n) for n in ("f1a", "f1b", "f1c")]
        wpslots = [Slot(S, n) for n in ("wpa", "wpb")]

        dump_n = [0]

        def dump(name, src, after):
            if not debug:
                return None
            sem = f"d_dbg{dump_n[0]}"
            dump_n[0] += 1
            return S.op("gpsimd", lambda e, d=d_dbg[name], sv=src: e.dma_start(
                out=d, in_=sv), after=after, dma=sem)

        t_const = None
        for (dst, src) in [(MW[:], d_maskw), (IDB[:], d_idb), (IDF[:], d_idf),
                           (NGT[:], d_ng)]:
            t_const = S.op("sync", lambda e, d=dst, s=src: e.dma_start(out=d, in_=s),
                           dma="d_const")
        S.op("vector", lambda e: e.memset(ONESC[:], 1.0))
        S.op("vector", lambda e: e.memset(CNEG[:], -16.0))
        t_ngc = S.op("vector", lambda e: e.tensor_scalar(
            NGC[:], NGT[:], float(np.exp(-16.0)), None, OP.mult),
            after=[t_const], tok=True)
        t_ta0 = S.op("vector", lambda e: e.memset(TA[:], 0.0), tok=True)
        S.op("vector", lambda e: e.memset(ONESPB[:], 1.0))
        t_ones = S.op("vector", lambda e: e.memset(ONESPF[:], 1.0), tok=True)

        gamma_state = {"i": 0, "last": []}

        def emit_gamma(n):
            for _ in range(n):
                j = gamma_state["i"]
                if j >= 18:
                    return
                gamma_state["i"] = j + 1
                sidx = j % 3
                slot = f1slots[sidx]
                aft2 = list(slot.free)
                tl = slot.load("sync", [lambda e, sx=sidx, jj=j: e.dma_start(
                    out=F1[sx][:, 0:1024], in_=d_wsl[jj][:, 0:1024])])
                tl2 = S.op("sync", lambda e, sx=sidx, jj=j: e.dma_start(
                    out=F1[sx][:, 1024:2048], in_=d_wsl[jj][:, 1024:2048]),
                    after=aft2, dma=slot.sem + "x")
                if j % 2 == 0:
                    ta = S.op("scalar", lambda e, sx=sidx, c=j: e.activation(
                        F1[sx][:], F1[sx][:], AF.Abs, accum_out=GA[:, c:c + 1]),
                        after=[tl, tl2], tok=True)
                else:
                    ta = S.op("vector", lambda e, sx=sidx, c=j: e.reduce_sum(
                        GA[:, c:c + 1], F1[sx][:], AX, apply_absolute_value=True),
                        after=[tl, tl2], tok=True)
                slot.consumed(ta)
                gamma_state["last"].append(ta)

        ps_rot = [PS_C, PS_D]
        ps_free = {id(PS_C): None, id(PS_D): None}
        sc_slot_free = [None, None]
        t_xl1_done = None
        t_xl1_half = [None, None]
        t_xfm_done = None
        prev_chain = [None, None]
        for t in range(16):
            own = t >= 8
            slot = f1slots[t % 3]
            sidx = t % 3
            src = d_xo if own else d_xa
            r0 = ((t - 8) if own else t) * 128
            aft2 = list(slot.free)
            tload = slot.load("sync", [lambda e, s=sidx, ss=src, r=r0:
                                       e.dma_start(
                                           out=F1[s][:, 0:1024],
                                           in_=ss[r:r + 128, 0:1024])])
            tl2 = S.op("sync", lambda e, s=sidx, ss=src, r=r0: e.dma_start(
                out=F1[s][:, 1024:2048], in_=ss[r:r + 128, 1024:2048]),
                after=aft2, dma=slot.sem + "x")
            tload = [tload, tl2]
            ftile = F1[sidx][:]
            t_red = S.op("vector", lambda e, ft=ftile, c=t: e.reduce_sum(
                ST[:, c:c + 1], ft, AX), after=list(tload), tok=True)
            snk = SCv[:, (t % 2) * 2048:(t % 2) * 2048 + 2048]
            nb0 = 6 * (t % 2)
            t_sq = S.op("scalar", lambda e, ft=ftile, c=t, sk=snk: e.activation(
                sk, ft, AF.Square, accum_out=ST[:, 16 + c:17 + c]),
                after=tload + [sc_slot_free[t % 2]], tok=True)
            t_c = S.op("vector", lambda e, c=t: e.tensor_scalar(
                ST[:, 32 + c:33 + c], ST[:, c:c + 1], 1.0 / H, None, OP.mult),
                after=[t_red, prev_chain[t % 2]], tok=True)
            t_c = S.op("vector", lambda e, c=t, n0=nb0: e.tensor_mul(
                NM[:, n0:n0 + 1], ST[:, 32 + c:33 + c], ST[:, 32 + c:33 + c]),
                after=[t_c], tok=True)
            t_c = S.op("vector", lambda e, c=t, n0=nb0: e.tensor_scalar(
                NM[:, n0 + 1:n0 + 2], ST[:, 16 + c:17 + c], 1.0 / H, None, OP.mult),
                after=[t_sq, t_c], tok=True)
            t_c = S.op("vector", lambda e, n0=nb0: e.tensor_sub(
                NM[:, n0 + 2:n0 + 3], NM[:, n0 + 1:n0 + 2], NM[:, n0:n0 + 1]),
                after=[t_c], tok=True)
            t_c = S.op("vector", lambda e, n0=nb0: e.tensor_scalar(
                NM[:, n0 + 2:n0 + 3], NM[:, n0 + 2:n0 + 3], EPS_LN, None, OP.add),
                after=[t_c], tok=True)
            t_c = S.op("vector", lambda e, n0=nb0: e.reciprocal(
                NM[:, n0 + 3:n0 + 4], NM[:, n0 + 2:n0 + 3]), after=[t_c], tok=True)
            t_s = S.op("scalar", lambda e, n0=nb0: e.activation(
                NM[:, n0 + 4:n0 + 5], NM[:, n0 + 3:n0 + 4], AF.Sqrt),
                after=[t_c], tok=True)
            t_c = S.op("vector", lambda e, n0=nb0: e.tensor_mul(
                NM[:, n0 + 5:n0 + 6], NM[:, n0 + 4:n0 + 5], NM[:, n0 + 4:n0 + 5]),
                after=[t_s], tok=True)
            t_c = S.op("vector", lambda e, n0=nb0: e.tensor_mul(
                NM[:, n0 + 5:n0 + 6], NM[:, n0 + 5:n0 + 6], NM[:, n0 + 2:n0 + 3]),
                after=[t_c], tok=True)
            t_c = S.op("vector", lambda e, n0=nb0: e.tensor_scalar(
                NM[:, n0 + 5:n0 + 6], NM[:, n0 + 5:n0 + 6], 3.0, -0.5,
                OP.subtract, OP.mult), after=[t_c], tok=True)
            t_c = S.op("vector", lambda e, c=t, n0=nb0: e.tensor_mul(
                ST[:, 48 + c:49 + c], NM[:, n0 + 4:n0 + 5], NM[:, n0 + 5:n0 + 6]),
                after=[t_c], tok=True)
            prev_chain[t % 2] = t_c
            xt_tm = snk
            t_norm = S.op("vector", lambda e, ft=ftile, c=t, dv=xt_tm: e.tensor_scalar(
                dv, ft, ST[:, 32 + c:33 + c], ST[:, 48 + c:49 + c],
                OP.subtract, OP.mult), after=[t_c, t_sq], tok=True)
            dst3 = A13 if own else A23
            tt_own = (t - 8) if own else t
            for r in range(4):
                ps = ps_rot[r % 2]
                trs = None
                for cc in range(4):
                    dc = 4 * r + cc
                    trs = S.op("tensor", lambda e, p=ps, d=dc, c=cc, sv=xt_tm:
                               e.transpose(
                                   p[:].bitcast(BF16)[:, c * 128:(c + 1) * 128],
                                   sv[:, d * 128:(d + 1) * 128], IDB[:]),
                               after=([t_norm, t_const, ps_free[id(ps)]]
                                      if cc == 0 else []), tok=(cc == 3))
                if r % 2 == 0:
                    ev = S.op("vector", lambda e, p=ps, rr=r, d3=dst3, tt=tt_own:
                              e.tensor_copy(
                                  d3[:, 4 * rr:4 * rr + 4, tt * 128:(tt + 1) * 128],
                                  p[:].bitcast(BF16)[:, 0:512].rearrange(
                                      "p (c u) -> p c u", c=4)),
                              after=[trs], tok=True)
                else:
                    ev = S.op("scalar", lambda e, p=ps, rr=r, d3=dst3, tt=tt_own:
                              e.activation(
                                  d3[:, 4 * rr:4 * rr + 4, tt * 128:(tt + 1) * 128],
                                  p[:].bitcast(BF16)[:, 0:512].rearrange(
                                      "p (c u) -> p c u", c=4), AF.Copy),
                              after=[trs], tok=True)
                ps_free[id(ps)] = ev
                if r % 2 == 0:
                    ev_d_ = ev
                else:
                    ev_a_ = ev
            t_xl1_done = [ev_d_, ev_a_]
            t_xl1_half[1 if own else 0] = t_xl1_done
            sc_slot_free[t % 2] = S.tok_now("e_tensor")
            if own:
                for r in range(4):
                    ps = ps_rot[r % 2]
                    trs = None
                    for cc in range(4):
                        dc = 4 * r + cc
                        trs = S.op("tensor", lambda e, p=ps, d=dc, c=cc, ft=ftile:
                                   e.transpose(p[:, c * 128:(c + 1) * 128],
                                               ft[:, d * 128:(d + 1) * 128], IDF[:]),
                                   after=((tload + [t_const, ps_free[id(ps)]])
                                          if cc == 0 else []), tok=(cc == 3))
                    if r % 2 == 0:
                        ev = S.op("vector", lambda e, p=ps, rr=r, tt=tt_own:
                                  e.tensor_copy(
                                      X3[:, 4 * rr:4 * rr + 4,
                                         tt * 128:(tt + 1) * 128],
                                      p[:, 0:512].rearrange("p (c u) -> p c u",
                                                            c=4)),
                                  after=[trs], tok=True)
                    else:
                        ev = S.op("scalar", lambda e, p=ps, rr=r, tt=tt_own:
                                  e.activation(
                                      X3[:, 4 * rr:4 * rr + 4,
                                         tt * 128:(tt + 1) * 128],
                                      p[:, 0:512].rearrange("p (c u) -> p c u",
                                                            c=4), AF.Copy),
                                  after=[trs], tok=True)
                    ps_free[id(ps)] = ev
                    if r % 2 == 0:
                        evx_d_ = ev
                    else:
                        evx_a_ = ev
                t_xfm_done = [evx_d_, evx_a_]
            slot.consumed(t_norm)
            if own:
                slot.consumed(S.tok_now("e_tensor"))
            if t >= 2:
                emit_gamma(2)

        t_dbg_xl1 = dump("dbg_xl1", A1v, list(t_xl1_done))


        def panel_load(slot_i, dw, ocol, after=()):
            slot = wpslots[slot_i]
            dst = WQPv[:, slot_i * 2048:(slot_i + 1) * 2048]
            dst3 = dst.rearrange("p (c o) -> p c o", c=16)
            fns = []
            for cc in range(4):
                sr = dw[cc * 512:(cc + 1) * 512, ocol:ocol + 128].rearrange(
                    "(c p) o -> p c o", p=128)
                fns.append(lambda e, s3=dst3, c4=cc, s=sr: e.dma_start(
                    out=s3[:, c4 * 4:(c4 + 1) * 4, :], in_=s))
            t = slot.load("gpsimd", fns, after=after)
            return slot, dst, t

        t_kv_done = None
        p1_panels = [(wi_, pj_) for wi_ in range(2) for pj_ in range(4)]
        p1_pre = [panel_load(i_ % 2, [d_wk, d_wv][p[0]], p[1] * 128)
                  for i_, p in enumerate(p1_panels[:2])]
        emit_gamma(18)
        ga_last = gamma_state["last"]
        ga_act = [t for t in ga_last if t[0] == "e_scalar"]
        ga_dve = [t for t in ga_last if t[0] == "e_vector"]
        t_glast = ([ga_act[-1]] if ga_act else []) + ([ga_dve[-1]] if ga_dve else [])
        t_gmm = S.op("tensor", lambda e: e.matmul(
            PS_B[0:18, 0:1], GA[:, 0:18], ONESPF[:], start=True, stop=True),
            after=t_glast + [t_ones], tok=True)
        t_gcp = S.op("vector", lambda e: e.tensor_copy(G2[:], PS_B[0:18, 0:1]),
                     after=[t_gmm], tok=True)
        t_gst = S.op("sync", lambda e: e.dma_start(out=d_gsl, in_=G2[:]),
                     after=[t_gcp], dma="d_gst")
        t_cc1 = S.op("gpsimd", lambda e: e.collective_compute(
            "AllGather", OP.bypass, replica_groups=[list(range(8))],
            ins=[d_gsl.opt()], outs=[d_gfull.opt()]),
            after=[t_gst], inc=("cc1", 1))
        t_gld = S.op("sync", lambda e: e.dma_start(
            out=GF8[0:8, 0:18], in_=d_gfull.rearrange("(c j) o -> c (j o)", j=18)),
            after=[t_cc1], dma="d_gld")
        t_tau = []
        prev_tau_tok = None
        for mi in range(3):
            tg = S.op("vector", lambda e, m=mi: e.reduce_sum(
                GF8[0:8, 19:20], GF8[0:8, 6 * m:6 * m + 6],
                AX), after=[t_gld, prev_tau_tok], tok=True)
            tmm = S.op("tensor", lambda e, m=mi: e.matmul(
                PS_B[0:1, 500 + m:501 + m], GF8[0:8, 19:20], ONESPF[0:8, 0:1],
                start=True, stop=True), after=[tg], tok=True)
            tc = S.op("vector", lambda e, m=mi: e.tensor_scalar(
                G1[0:1, m:m + 1], PS_B[0:1, 500 + m:501 + m],
                1.0 / (FF * H), EPS_Q, OP.mult, OP.add), after=[tmm], tok=True)
            tc2 = S.op("vector", lambda e, m=mi: e.tensor_scalar(
                G1[0:1, 4 + m:5 + m], G1[0:1, m:m + 1], 0.5, None, OP.mult),
                after=[tc], tok=True)
            tc3 = S.op("vector", lambda e, m=mi: e.tensor_scalar(
                G1[0:1, 8 + m:9 + m], G1[0:1, m:m + 1], -0.5, None, OP.mult),
                after=[tc2], tok=True)
            tb = S.op("tensor", lambda e, m=mi: e.matmul(
                PS_B[:, 504 + m:505 + m], ONESC[:], G1[0:1, 4 + m:5 + m],
                start=True, stop=True), after=[tc3], tok=True)
            tb2 = S.op("tensor", lambda e, m=mi: e.matmul(
                PS_B[:, 508 + m:509 + m], ONESC[:], G1[0:1, 8 + m:9 + m],
                start=True, stop=True), after=[tb], tok=True)
            tv = S.op("vector", lambda e, m=mi: e.tensor_copy(
                TA[:, m:m + 1], PS_B[:, 504 + m:505 + m]), after=[tb2], tok=True)
            tv = S.op("vector", lambda e, m=mi: e.tensor_copy(
                TA[:, 4 + m:5 + m], PS_B[:, 508 + m:509 + m]), after=[tv], tok=True)
            t_tau.append(tv)
            prev_tau_tok = tv
        t_taud = t_tau[-1]

        q_state = {"i": 0, "prevadd": t_taud,
                   "sthome": [t_taud, t_taud], "pend": None,
                   "stl": [], "ccs": []}

        def _emit_pend():
            pe = q_state["pend"]
            if pe is None:
                return
            q_state["pend"] = None
            jj_, av_, sx_, tadd_ = pe
            tst = S.op("scalar", lambda e, jj=jj_, av=av_: e.dma_start(
                out=d_qsl[jj], in_=av), after=[tadd_], dma=f"d_qs{sx_}")
            q_state["sthome"][jj_ % 2] = tst
            q_state["stl"].append((jj_, tst))
            if jj_ in (5, 11):
                m3 = jj_ // 6
                in_m = d_qsl[6 * m3:6 * m3 + 6].opt()
                out_m = d_qfull[48 * m3:48 * m3 + 48].opt()
                aft_m = [tok for (j2, tok) in q_state["stl"]
                         if 6 * m3 <= j2 < 6 * m3 + 6][-3:]
                q_state["ccs"].append(S.op(
                    "gpsimd", lambda e, i_=in_m, o_=out_m:
                    e.collective_compute(
                        "AllGather", OP.bypass,
                        replica_groups=[list(range(8))],
                        ins=[i_], outs=[o_]),
                    after=aft_m, inc=("cc2", 1)))

        def emit_quant(n):
            for _ in range(n):
                j = q_state["i"]
                if j >= 18:
                    return
                q_state["i"] = j + 1
                m = j // 6
                sidx = j % 3
                slot = f1slots[sidx]
                aft2 = list(slot.free)
                tl = slot.load("sync", [lambda e, sx=sidx, jj=j: e.dma_start(
                    out=F1[sx][:, 0:1024], in_=d_wsl[jj][:, 0:1024])])
                tl2 = S.op("sync", lambda e, sx=sidx, jj=j: e.dma_start(
                    out=F1[sx][:, 1024:2048], in_=d_wsl[jj][:, 1024:2048]),
                    after=aft2, dma=slot.sem + "x")
                _emit_pend()
                h = j % 2
                a_ = SCv[:, h * 2048:h * 2048 + 2048]
                b_ = SCv[:, (1 - h) * 2048:(1 - h) * 2048 + 2048]
                tsa = S.op("scalar", lambda e, sx=sidx, mm_=m, av=a_:
                           e.activation(
                               av, F1[sx][:], AF.Sign,
                               bias=TA[:, 4 + mm_:5 + mm_]),
                           after=[tl, tl2, t_tau[m], q_state["prevadd"],
                                  q_state["sthome"][h]], tok=True)
                tsb = S.op("scalar", lambda e, sx=sidx, mm_=m, bv=b_:
                           e.activation(
                               bv, F1[sx][:], AF.Sign, bias=TA[:, mm_:mm_ + 1]),
                           after=[tsa, q_state["sthome"][1 - h]], tok=True)
                slot.consumed(tsb)
                tadd = S.op("vector", lambda e, av=a_, bv=b_: e.tensor_tensor(
                    av, av, bv, OP.add), after=[tsb], tok=True)
                q_state["prevadd"] = tadd
                q_state["pend"] = (j, a_, sidx, tadd)

        for p1i in range(8):
            wi, pj = p1_panels[p1i]
            if True:
                slot, pan, t_pan = p1_pre[p1i]
                t_last_mm = None
                for tc in range(4):
                    src_ar = A2v if tc < 2 else A1v
                    th = tc % 2
                    ps = ps_rot[tc % 2]
                    mm = None
                    for dc in range(16):
                        mm = S.op("tensor", lambda e, p=ps, pn=pan, d=dc, sa=src_ar,
                                  t2=th: e.matmul(
                                      p[:], pn[:, d * 128:(d + 1) * 128],
                                      sa[:, d * 1024 + t2 * 512:
                                         d * 1024 + t2 * 512 + 512],
                                      start=(d == 0), stop=(d == 15)),
                                  after=(([t_pan, ps_free[id(ps)]] +
                                          t_xl1_half[0 if tc < 2 else 1])
                                         if dc == 0 else []), tok=(dc == 15))
                    t_last_mm = mm
                    dst = (A3v if wi == 0 else A4v)[
                        :, pj * 2048 + tc * 512: pj * 2048 + tc * 512 + 512]
                    ev = S.op("vector", lambda e, p=ps, d=dst: e.tensor_copy(
                        d, p[:]), after=[mm], tok=True)
                    ps_free[id(ps)] = ev
                    t_kv_done = ev
                slot.consumed(t_last_mm)
                if p1i + 2 < 8:
                    pn_ = p1_panels[p1i + 2]
                    p1_pre.append(panel_load((p1i + 2) % 2,
                                             [d_wk, d_wv][pn_[0]],
                                             pn_[1] * 128))
                emit_quant(1)
        t_vtm_all = None
        for kvh in range(KVH):
            for kt in range(16):
                base = kvh * 2048 + kt * 128
                ps = ps_rot[kt % 2]
                tr = S.op("tensor", lambda e, p=ps, b=base: e.transpose(
                    p[:].bitcast(BF16)[:, 0:128], A4v[:, b:b + 128], IDB[:]),
                    after=[t_kv_done, ps_free[id(ps)]], tok=True)
                ev = S.op("vector", lambda e, p=ps, b=base: e.tensor_copy(
                    A4v[:, b:b + 128], p[:].bitcast(BF16)[:, 0:128]),
                    after=[tr], tok=True)
                ps_free[id(ps)] = ev
                t_vtm_all = ev

        t_dbg_k = dump("dbg_k", A3v, [t_kv_done])
        t_dbg_v = dump("dbg_v", A4v, [t_vtm_all])

        t_q_done = None
        for pj in range(16):
            slot, pan, t_pan = panel_load(pj % 2, d_wq, pj * 128)
            t_last_mm = None
            for th in range(2):
                ps = ps_rot[th % 2]
                mm = None
                for dc in range(16):
                    mm = S.op("tensor", lambda e, p=ps, pn=pan, d=dc, t2=th:
                              e.matmul(p[:], pn[:, d * 128:(d + 1) * 128],
                                       A1v[:, d * 1024 + t2 * 512:
                                           d * 1024 + t2 * 512 + 512],
                                       start=(d == 0), stop=(d == 15)),
                              after=([t_pan, t_kv_done, ps_free[id(ps)]]
                                     if dc == 0 else []), tok=(dc == 15))
                t_last_mm = mm
                dst = A2v[:, pj * 1024 + th * 512: pj * 1024 + th * 512 + 512]
                ev = S.op("vector", lambda e, p=ps, d=dst: e.tensor_scalar(
                    d, p[:], QSCALE, None, OP.mult), after=[mm], tok=True)
                ps_free[id(ps)] = ev
                t_q_done = ev
            slot.consumed(t_last_mm)
            emit_quant(1)

        t_dbg_q = dump("dbg_q", A2v, [t_q_done])

        emit_quant(18)
        _emit_pend()
        in_m = d_qsl[12:18].opt()
        out_m = d_qfull[96:144].opt()
        aft_m = [tok for (jj2, tok) in q_state["stl"] if 12 <= jj2 < 18][-3:]
        t_cc2 = S.op("gpsimd", lambda e, i_=in_m, o_=out_m:
                     e.collective_compute(
                         "AllGather", OP.bypass,
                         replica_groups=[list(range(8))],
                         ins=[i_], outs=[o_]),
                     after=aft_m, inc=("cc2", 1))
        t_q_sc0, t_q_sc1 = q_state["sthome"]

        sc_rot = [PS_A, PS_B]
        sc_free = {id(PS_A): [t_taud], id(PS_B): [t_taud]}
        avst = {"psfree": ps_free[id(PS_C)], "mm": None, "all": None}
        p_free = [t_q_done, t_q_done]
        nm_free = [None, None]
        pending_av = []
        oproj_jobs = []
        oproj_state = {"load_i": 0, "mm_i": 0, "done": None, "ocd": {}}
        wo_slots = [Slot(S, f"wo{i}") for i in range(3)]
        for sl_ in wo_slots:
            sl_.consumed(t_q_done)
        wov = [WQPv[:, 2560 + i * 512:2560 + (i + 1) * 512] for i in range(3)]
        opps_free = {id(PS_C): None, id(PS_D): None}

        def pump_oproj(n):
            for _ in range(n):
                li = oproj_state["load_i"]
                if li < len(oproj_jobs) and li - oproj_state["mm_i"] < 5:
                    kv_, oc_, th_, _avtok = oproj_jobs[li]
                    if th_ == 0:
                        sl = wo_slots[(li // 2) % 3]
                        src = d_wo[(4 * kv_) * 128:(4 * kv_ + 4) * 128,
                                   oc_ * 128:(oc_ + 1) * 128].rearrange(
                                       "(c p) o -> p c o", p=128)
                        dst3 = wov[(li // 2) % 3].rearrange(
                            "p (c o) -> p c o", c=4)
                        sl.load("gpsimd", [lambda e, d3=dst3, s_=src:
                                           e.dma_start(out=d3, in_=s_)])
                    oproj_state["load_i"] = li + 2 if th_ == 0 else li + 1
                mi = oproj_state["mm_i"]
                if mi >= len(oproj_jobs) or mi >= oproj_state["load_i"]:
                    continue
                kv_, oc_, th_, avtok = oproj_jobs[mi]
                oproj_state["mm_i"] = mi + 1
                sl = wo_slots[(mi // 2) % 3]
                pan = wov[(mi // 2) % 3]
                ps = ps_rot[mi % 2]
                mm = None
                for cc in range(4):
                    mm = S.op("tensor", lambda e, p=ps, pn=pan, c=cc, kv2=kv_,
                              t2=th_: e.matmul(
                                  p[:], pn[:, c * 128:(c + 1) * 128],
                                  A1v[:, (4 * kv2 + c) * 1024 + t2 * 512:
                                      (4 * kv2 + c) * 1024 + t2 * 512 + 512],
                                  start=(c == 0), stop=(c == 3)),
                              after=(([sl.ready, avtok, opps_free[id(ps)],
                                       av_free[id(ps)]] + list(t_xfm_done))
                                     if cc == 0 else []),
                              tok=(cc == 3))
                if th_ == 1:
                    sl.consumed(mm)
                xsl = Xv[:, oc_ * 1024 + th_ * 512:oc_ * 1024 + th_ * 512 + 512]
                ev2 = S.op("vector", lambda e, p=ps, xs=xsl: e.tensor_tensor(
                    xs, p[:], xs, OP.add),
                    after=[mm, oproj_state["xw"]], tok=True)
                opps_free[id(ps)] = ev2
                oproj_state["xw"] = ev2
                oproj_state["done"] = ev2
                if kv_ == 3:
                    oproj_state["ocd"][(oc_, th_)] = ev2
        oproj_state["xw"] = None
        pending_trans = [None]
        pT_toks = []
        pT_prev = []

        av_free = {id(PS_C): ps_free[id(PS_C)],
                   id(PS_D): ps_free[id(PS_D)]}

        def emit_av(kv, q, my_toks):
            psd = [PS_C, PS_D][q % 2]
            mm = None
            for w in range(9):
                kt = q + w
                mm = S.op("tensor", lambda e, p=psd, kv_=kv, k=kt, ww=w:
                          e.matmul(
                              p[:, 0:512],
                              A4v[:, (kv_ * 16 + k) * 128:
                                  (kv_ * 16 + k) * 128 + 128],
                              SCv[:, ww * 512:ww * 512 + 512],
                              start=(ww == 0), stop=(ww == 8)),
                          after=(my_toks + [t_vtm_all, av_free[id(psd)],
                                            opps_free[id(psd)]]
                                 if w == 0 else []),
                          tok=(w == 8))
            ev = S.op("scalar", lambda e, kv_=kv, q_=q, p=psd: e.activation(
                A1v[:, kv_ * 4096:(kv_ + 1) * 4096].rearrange(
                    "p (c u) -> p c u", c=4)[:, :, q_ * 128:(q_ + 1) * 128],
                p[:, 0:512].rearrange("p (c u) -> p c u", c=4), AF.Copy),
                after=[mm, t_dbg_xl1], tok=True)
            av_free[id(psd)] = ev
            avst["psfree"] = ev
            avst["mm"] = mm
            avst["all"] = ev
            if q == 7:
                for oc_ in range(16):
                    for th_ in range(2):
                        oproj_jobs.append((kv, oc_, th_, ev))

        for kvh in range(KVH):
            for qt in range(8):
                for qh2 in range(4):
                    qh = kvh * 4 + qh2
                    par = qh2 % 2
                    nb = 8 * par
                    scp = sc_rot[par]
                    pbuf = WQPv[:, par * 1152:par * 1152 + 1152]
                    kbase = kvh * 2048 + qt * 128
                    qcol = qh * 1024 + qt * 128
                    sf = sc_free[id(scp)]
                    sf = sf if isinstance(sf, list) else [sf]
                    S.op("tensor", lambda e, p=scp: e.matmul(
                        p[:, 0:128], IDB[:], MW[:, 0:128], start=True, stop=False),
                        after=[t_q_done, t_vtm_all, t_const] + sf)
                    S.op("tensor", lambda e, p=scp: e.matmul(
                        p[:, 1024:1152], IDB[:], MW[:, 512:640],
                        start=True, stop=False))
                    S.op("tensor", lambda e, p=scp, q=qcol, k=kbase: e.matmul(
                        p[:, 0:128], A2v[:, q:q + 128], A3v[:, k:k + 128],
                        start=False, stop=True))
                    S.op("tensor", lambda e, p=scp, q=qcol, k=kbase: e.matmul(
                        p[:, 128:512], A2v[:, q:q + 128], A3v[:, k + 128:k + 512],
                        start=True, stop=True))
                    S.op("tensor", lambda e, p=scp, q=qcol, k=kbase: e.matmul(
                        p[:, 512:1024], A2v[:, q:q + 128], A3v[:, k + 512:k + 1024],
                        start=True, stop=True))
                    t_sc = S.op("tensor", lambda e, p=scp, q=qcol, k=kbase: e.matmul(
                        p[:, 1024:1152], A2v[:, q:q + 128], A3v[:, k + 1024:k + 1152],
                        start=False, stop=True), tok=True)
                    t_exp = S.op("scalar", lambda e, p=scp, pb=pbuf, n=nb:
                                 e.activation(pb, p[:, 0:1152], AF.Exp,
                                              bias=CNEG[:],
                                              accum_out=NM[:, n:n + 1]),
                                 after=[t_sc, p_free[par], nm_free[par]], tok=True)
                    if pending_trans[0] is not None:
                        pT_prev.append(pending_trans[0]())
                        pending_trans[0] = None
                    if qh2 == 0:
                        for fn_ in pending_av:
                            fn_()
                        pending_av = []
                        pT_toks = pT_prev
                        pT_prev = []
                    t_z = S.op("vector", lambda e, q=qt, n=nb: e.tensor_tensor(
                        NM[:, n + 1:n + 2], NM[:, n:n + 1], NGC[:, q:q + 1],
                        OP.subtract), after=[t_exp, t_ngc], tok=True)
                    t_z = S.op("vector", lambda e, n=nb: e.reciprocal(
                        NM[:, n + 2:n + 3], NM[:, n + 1:n + 2]), after=[t_z],
                        tok=True)
                    t_nrm = S.op("vector", lambda e, pb=pbuf, n=nb: e.tensor_scalar(
                        pb, pb, NM[:, n + 2:n + 3], None, OP.mult),
                        after=[t_z], tok=True)
                    nm_free[par] = t_nrm

                    def mk_trans(scp_=scp, pbuf_=pbuf, par_=par, qh2_=qh2,
                                 t_nrm_=t_nrm, t_exp_=t_exp, kvh_=kvh, qt_=qt):
                        def emit_trans():
                            trs = None
                            for w in range(9):
                                trs = S.op("tensor", lambda e, p=scp_, pb=pbuf_,
                                           ww=w: e.transpose(
                                               p[:].bitcast(BF16)
                                               [:, ww * 128:(ww + 1) * 128],
                                               pb[:, ww * 128:(ww + 1) * 128],
                                               IDB[:]),
                                           after=([t_nrm_, t_exp_] if w == 0
                                                  else []), tok=(w == 8))
                            p3g = ([t_q_sc0, t_q_sc1]
                                   if (kvh_ == 0 and qt_ == 0 and qh2_ == 0)
                                   else [])
                            ev_b = S.op("vector", lambda e, p=scp_, q2=qh2_:
                                        e.tensor_copy(
                                            SCv[:, 0:4608].rearrange(
                                                "p (w u) -> p w u", w=9)
                                            [:, :, q2 * 128:(q2 + 1) * 128],
                                            p[:].bitcast(BF16)[:, 0:1152]
                                            .rearrange("p (w u) -> p w u",
                                                       w=9)),
                                        after=[trs, avst["mm"]] + p3g, tok=True)
                            sc_free[id(scp_)] = [ev_b]
                            p_free[par_] = trs
                            return ev_b
                        return emit_trans

                    pending_trans[0] = mk_trans()
                    if qh2 >= 2:
                        pump_oproj(2)
                pending_av.append(
                    (lambda kv=kvh, q=qt: emit_av(kv, q, list(pT_toks) +
                                                  list(pT_prev) +
                                                  [S.tok_now("e_vector")])))
        if pending_trans[0] is not None:
            pT_prev.append(pending_trans[0]())
            pending_trans[0] = None
        pT_toks = pT_toks + pT_prev
        for fn_ in pending_av:
            fn_()
        pending_av = []
        t_av_all = avst["all"]
        av_mm_prev = avst["mm"]
        for i_ in range(2):
            wpslots[i_].consumed(av_mm_prev)

        ocd = oproj_state["ocd"]
        cast_free = {(h_, c_): t_av_all for h_ in range(2) for c_ in range(2)}
        mmx_h = [None, None]
        mmq_h = [None, None]
        sc_a = sc_free[id(PS_A)]
        sc_b = sc_free[id(PS_B)]
        sc_a = sc_a if isinstance(sc_a, list) else [sc_a]
        sc_b = sc_b if isinstance(sc_b, list) else [sc_b]
        for dc in range(16):
            pump_oproj(2)
            for hh2 in range(2):
                cslot = dc % 2
                xb = SCv[:, hh2 * 1024 + cslot * 512:
                         hh2 * 1024 + cslot * 512 + 512]
                xb2 = SCv[:, 2048 + hh2 * 1024 + cslot * 512:
                          2048 + hh2 * 1024 + cslot * 512 + 512]
                tcast = S.op("gpsimd", lambda e, d=dc, h2=hh2, b=xb: e.dma_start(
                    out=b,
                    in_=Xv[:, d * 1024 + h2 * 512:d * 1024 + h2 * 512 + 512]),
                    after=[ocd[(dc, hh2)], cast_free[(hh2, cslot)]],
                    dma=f"d_cast{2 * hh2 + cslot}")
                tsq = S.op("scalar", lambda e, b=xb, b2=xb2: e.activation(
                    b2, b, AF.Square),
                    after=[tcast, cast_free[(hh2, cslot)]], tok=True)
                mmx_h[hh2] = S.op("tensor", lambda e, b=xb, d=dc, h2=hh2:
                                  e.matmul(
                                      PS_A[0:1, h2 * 512:h2 * 512 + 512],
                                      ONESPB[:], b,
                                      start=(d == 0), stop=(d == 15)),
                                  after=([tcast, t_ones] +
                                         (sc_a if dc == 0 else [])),
                                  tok=True)
                mmq_h[hh2] = S.op("tensor", lambda e, b2=xb2, d=dc, h2=hh2:
                                  e.matmul(
                                      PS_B[0:1, h2 * 512:h2 * 512 + 512],
                                      ONESPB[:], b2,
                                      start=(d == 0), stop=(d == 15)),
                                  after=([tsq] + (sc_b if dc == 0 else [])),
                                  tok=True)
                cast_free[(hh2, cslot)] = mmq_h[hh2]
        pump_oproj(8)
        t_oproj_done = oproj_state["done"]
        x_write_tok = oproj_state["xw"]
        ps_free[id(PS_C)] = opps_free[id(PS_C)]
        ps_free[id(PS_D)] = opps_free[id(PS_D)]

        t_dbg_av = dump("dbg_av", A1v, [t_av_all])
        t_dbg_xattn = dump("dbg_xattn", Xv, [t_oproj_done])

        t_xl2_done = None
        stats_f1_guard = list(f1slots[1].free)
        f1slots[1].free = []
        for hh in range(2):
            hs = hh * 512
            m_sb = F1[1][0:1, 0:512]
            q2_sb = F1[1][0:1, 512:1024]
            t0_sb = F1[1][0:1, 1024:1536]
            a_sb = F1[1][0:1, 1536:2048]
            u_sb = F1[2][0:1, 512:1024]
            r0_sb = F1[2][0:1, 1024:1536]
            tc = S.op("vector", lambda e, h2=hh: e.tensor_scalar(
                m_sb, PS_A[0:1, h2 * 512:h2 * 512 + 512], 1.0 / H, None,
                OP.mult),
                after=[mmx_h[hh], mmq_h[hh], t_xl2_done] + stats_f1_guard,
                tok=True)
            tc = S.op("vector", lambda e, h2=hh: e.tensor_scalar(
                q2_sb, PS_B[0:1, h2 * 512:h2 * 512 + 512], 1.0 / H, None,
                OP.mult), after=[tc], tok=True)
            tc = S.op("vector", lambda e: e.tensor_mul(
                t0_sb, m_sb, m_sb), after=[tc], tok=True)
            tc = S.op("vector", lambda e: e.tensor_sub(
                a_sb, q2_sb, t0_sb), after=[tc], tok=True)
            tc = S.op("vector", lambda e: e.tensor_scalar(
                a_sb, a_sb, EPS_LN, None, OP.add), after=[tc], tok=True)
            tc = S.op("vector", lambda e: e.reciprocal(
                u_sb, a_sb), after=[tc], tok=True)
            ts_ = S.op("scalar", lambda e: e.activation(
                r0_sb, u_sb, AF.Sqrt), after=[tc], tok=True)
            tc = S.op("vector", lambda e: e.tensor_mul(
                t0_sb, r0_sb, r0_sb), after=[ts_], tok=True)
            tc = S.op("vector", lambda e: e.tensor_mul(
                t0_sb, t0_sb, a_sb), after=[tc], tok=True)
            tc = S.op("vector", lambda e: e.tensor_scalar(
                t0_sb, t0_sb, 3.0, -0.5, OP.subtract, OP.mult),
                after=[tc], tok=True)
            tA = S.op("vector", lambda e: e.tensor_mul(
                ROW[0:1, 0:512], r0_sb, t0_sb), after=[tc], tok=True)
            tbA = S.op("tensor", lambda e, h2=hh: e.matmul(
                PS_B[:, h2 * 512:h2 * 512 + 512], ONESC[:], ROW[0:1, 0:512],
                start=True, stop=True), after=[tA], tok=True)
            tB = S.op("vector", lambda e: e.tensor_mul(
                t0_sb, m_sb, ROW[0:1, 0:512]), after=[tbA], tok=True)
            tB = S.op("vector", lambda e: e.tensor_scalar(
                ROW[0:1, 0:512], t0_sb, -1.0, None, OP.mult),
                after=[tB], tok=True)
            tbB = S.op("tensor", lambda e, h2=hh: e.matmul(
                PS_A[:, h2 * 512:h2 * 512 + 512], ONESC[:], ROW[0:1, 0:512],
                start=True, stop=True), after=[tB], tok=True)
            stats_f1_guard = []
            t_n = None
            for dc in range(16):
                tmp = F1[2][:]
                t1 = S.op("vector", lambda e, d=dc, h2=hh, tp=tmp:
                          e.tensor_tensor(
                              tp[:, 0:512],
                              Xv[:, d * 1024 + h2 * 512:
                                 d * 1024 + h2 * 512 + 512],
                              PS_B[:, h2 * 512:h2 * 512 + 512], OP.mult),
                          after=([tbB, t_n] +
                                 (list(f1slots[2].free) if dc == 0 else [])),
                          tok=True)
                t_n = S.op("vector", lambda e, d=dc, h2=hh, tp=tmp:
                           e.tensor_tensor(
                               A2v[:, d * 1024 + h2 * 512:
                                   d * 1024 + h2 * 512 + 512],
                               tp[:, 0:512],
                               PS_A[:, h2 * 512:h2 * 512 + 512], OP.add),
                           after=[t1, t_dbg_q], tok=True)
            if hh == 0:
                f1slots[2].free = []
            t_xl2_done = t_n
        f1slots[2].consumed(t_xl2_done)
        f1slots[1].consumed(t_xl2_done)

        t_dbg_xl2 = dump("dbg_xl2", A2v, [t_xl2_done])
        t_dbg_tau = dump("dbg_tau", TA[:], t_tau)

        x1_free = t_xl2_done
        x2_free = t_xl2_done
        qw3_free = [[t_av_all, t_dbg_k, t_dbg_v] for _ in range(G3)]
        hid_free = [[t_oproj_done, t_dbg_av] for _ in range(G3)]
        qw12_free = [[t_oproj_done] + list(wpslots[0].free),
                     [t_oproj_done] + list(wpslots[1].free)]
        sg_free = None
        x3ps_free = {id(PS_C): ps_free[id(PS_C)], id(PS_D): ps_free[id(PS_D)]}
        hid_ready_g = []
        q3_toks = []
        t_cc2tok = t_cc2

        def gidx(mat, i):
            return 48 * mat + i

        for i in range(NC_F):
            g, ig = i // G3, i % G3
            qw1 = WQPv[:, 0:2048]
            qw2 = WQPv[:, 2048:4096]
            t_l1 = S.op("sync", lambda e, ii=i: e.dma_start(
                out=qw1, in_=d_qfull[gidx(0, ii)]),
                after=qw12_free[0] + [t_cc2tok], dma="qw1l")
            t_l2 = S.op("sync", lambda e, ii=i: e.dma_start(
                out=qw2, in_=d_qfull[gidx(1, ii)]),
                after=qw12_free[1] + [t_cc2tok], dma="qw2l")
            if ig < 4:
                qw3 = A3v[:, ig * 2048:(ig + 1) * 2048]
            else:
                qw3 = A4v[:, (ig - 4) * 2048:(ig - 4 + 1) * 2048]
            t_l3 = S.op("sync", lambda e, ii=i, q3=qw3: e.dma_start(
                out=q3, in_=d_qfull[gidx(2, ii)]),
                after=list(qw3_free[ig]) + [t_cc2tok], dma=f"q3l{ig}")
            q3_toks.append(t_l3)

            mm1 = None
            for th in range(2):
                for dc in range(16):
                    mm1 = S.op("tensor", lambda e, d=dc, t2=th: e.matmul(
                        PS_A[:, t2 * 512:t2 * 512 + 512],
                        qw1[:, d * 128:(d + 1) * 128],
                        A2v[:, d * 1024 + t2 * 512:d * 1024 + t2 * 512 + 512],
                        start=(d == 0), stop=(d == 15)),
                        after=([t_l1, t_xl2_done, x1_free] if (th == 0 and dc == 0)
                               else []), tok=(th == 1 and dc == 15))
            qw12_free[0] = [mm1]
            mm2 = None
            for th in range(2):
                for dc in range(16):
                    mm2 = S.op("tensor", lambda e, d=dc, t2=th: e.matmul(
                        PS_B[:, t2 * 512:t2 * 512 + 512],
                        qw2[:, d * 128:(d + 1) * 128],
                        A2v[:, d * 1024 + t2 * 512:d * 1024 + t2 * 512 + 512],
                        start=(d == 0), stop=(d == 15)),
                        after=([t_l2, x2_free] if (th == 0 and dc == 0) else []),
                        tok=(th == 1 and dc == 15))
            qw12_free[1] = [mm2]

            hidv = A1v[:, ig * 1024:(ig + 1) * 1024]
            t_hid = None
            tsg = None
            for th in range(2):
                sg = SCv[:, 4096:4608]
                tsg = S.op("scalar", lambda e, t2=th, s2=sg: e.activation(
                    s2, PS_A[:, t2 * 512:t2 * 512 + 512], AF.Sigmoid,
                    scale=0.5), after=[mm1, sg_free], tok=True)
                th1 = S.op("vector", lambda e, t2=th, h=hidv: e.tensor_scalar(
                    h[:, t2 * 512:t2 * 512 + 512],
                    PS_B[:, t2 * 512:t2 * 512 + 512], 0.25, None, OP.mult),
                    after=[mm2] + list(hid_free[ig] or []), tok=True)
                t_hid = S.op("vector", lambda e, t2=th, h=hidv, s2=sg:
                             e.tensor_tensor(h[:, t2 * 512:t2 * 512 + 512],
                                             h[:, t2 * 512:t2 * 512 + 512],
                                             s2, OP.mult),
                             after=[tsg, th1], tok=True)
                sg_free = t_hid
            x1_free = tsg
            x2_free = t_hid
            hid_free[ig] = []
            hid_ready_g.append(t_hid)

            if (ig == G3 - 1) or (i == NC_F - 1):
                n_in_g = ig + 1
                hid_ready = hid_ready_g[-1]
                ld_toks = q3_toks[-n_in_g:]
                t_grp = None
                for dc in range(16):
                    for th in range(2):
                        ps = ps_rot[(dc * 2 + th) % 2]
                        mm = None
                        for j in range(n_in_g):
                            if j < 4:
                                q3 = A3v[:, j * 2048 + dc * 128:
                                         j * 2048 + dc * 128 + 128]
                            else:
                                q3 = A4v[:, (j - 4) * 2048 + dc * 128:
                                         (j - 4) * 2048 + dc * 128 + 128]
                            hj = A1v[:, j * 1024 + th * 512:
                                     j * 1024 + th * 512 + 512]
                            mm = S.op("tensor", lambda e, p=ps, qq=q3, hh_=hj,
                                      j2=j, ng=n_in_g: e.matmul(
                                          p[:], qq, hh_, start=(j2 == 0),
                                          stop=(j2 == ng - 1)),
                                      after=([hid_ready, x3ps_free[id(ps)]]
                                             + ld_toks if j == 0
                                             else []), tok=(j == n_in_g - 1))
                        xsl = Xv[:, dc * 1024 + th * 512:dc * 1024 + th * 512 + 512]
                        ev = S.op("vector", lambda e, p=ps, xs=xsl: e.tensor_tensor(
                            xs, p[:], xs, OP.add),
                            after=[mm, x_write_tok, t_dbg_xattn], tok=True)
                        x3ps_free[id(ps)] = ev
                        t_grp = ev
                x_write_tok = t_grp
                for k in range(G3):
                    qw3_free[k] = [t_grp]
                    hid_free[k] = [t_grp]
                hid_ready_g = []

        t_ffn_done = x_write_tok

        t_out_sem_total = None
        for tt in range(8):
            sidx = tt % 3
            slot = f1slots[sidx]
            evs = []
            slot_guard = list(slot.free)
            slot.free = []
            for r in range(4):
                ps = ps_rot[r % 2]
                trs = None
                for cc in range(4):
                    dc = 4 * r + cc
                    trs = S.op("tensor", lambda e, p=ps, d=dc, c=cc, t2=tt:
                               e.transpose(p[:, c * 128:(c + 1) * 128],
                                           Xv[:, d * 1024 + t2 * 128:
                                              d * 1024 + t2 * 128 + 128], IDF[:]),
                               after=([t_ffn_done, ps_free[id(ps)],
                                       x3ps_free[id(ps)]] + slot_guard
                                      if cc == 0 else []), tok=(cc == 3))
                ev = S.op("vector", lambda e, p=ps, rr=r, s=sidx: e.tensor_copy(
                    F1[s][:, rr * 512:(rr + 1) * 512], p[:, 0:512]),
                    after=[trs] + (slot_guard if r == 0 else []), tok=True)
                ps_free[id(ps)] = ev
                x3ps_free[id(ps)] = ev
                evs.append(ev)
            t_st = S.op("sync", lambda e, s=sidx, t2=tt: e.dma_start(
                out=d_out[t2 * 128:(t2 + 1) * 128, :], in_=F1[s][:]),
                after=evs, dma=f"d_out{sidx}")
            slot.consumed(t_st)

        total_out = [S.cnt.get(f"d_out{i}", 0) for i in range(3)]

        @block.sync
        def _(sync):
            S.emit("sync", sync, sems)

        @block.gpsimd
        def _(gpsimd):
            S.emit("gpsimd", gpsimd, sems)
            for i_ in range(3):
                gpsimd.wait_ge(sems[f"d_out{i_}"], total_out[i_])
            for i_ in range(dump_n[0]):
                gpsimd.wait_ge(sems[f"d_dbg{i_}"], 16)

        @block.tensor
        def _(tensor):
            S.emit("tensor", tensor, sems)

        @block.vector
        def _(vector):
            S.emit("vector", vector, sems)

        @block.scalar
        def _(scalar):
            S.emit("scalar", scalar, sems)

    return nc


def _host_consts():
    ii = np.arange(128)[:, None]
    maskw = np.zeros((128, 640), np.float32)
    uu = np.arange(128)[None, :]
    maskw[:, 0:128] = np.where(uu >= ii, 0.0, NEG)
    maskw[:, 512:640] = np.where(uu <= ii, 0.0, NEG)
    maskw = maskw.astype(ml_dtypes.bfloat16)
    identb = np.eye(128, dtype=ml_dtypes.bfloat16)
    identf = np.eye(128, dtype=np.float32)
    return maskw, identb, identf


def _ng_for(h):
    ng = np.zeros((128, 8), np.float32)
    if h == 0:
        ii = np.arange(128)[:, None]
        for qt in range(8):
            ng[:, qt:qt + 1] = 128.0 * (8 - qt) - ii
    return ng


_NC_CACHE = {}


def _get_nc(debug=False):
    if debug not in _NC_CACHE:
        _NC_CACHE[debug] = build_nc(debug=debug)
    return _NC_CACHE[debug]


def make_in_maps(inputs):
    x = np.asarray(inputs["L0_attn_hidden_states"], np.float32)
    maskw, identb, identf = _host_consts()
    zeros = np.zeros((NT_OWN, H), np.float32)
    common = {
        "wq": np.asarray(inputs["L0_attn_Wq"], np.float32),
        "wk": np.asarray(inputs["L0_attn_Wk"], np.float32),
        "wv": np.asarray(inputs["L0_attn_Wv"], np.float32),
        "wo": np.asarray(inputs["L0_attn_Wo"], np.float32),
        "maskw": maskw, "identb": identb, "identf": identf,
    }
    w1 = np.asarray(inputs["L0_ffn_w1"], np.float32)
    w2 = np.asarray(inputs["L0_ffn_w2"], np.float32)
    w3 = np.asarray(inputs["L0_ffn_w3"], np.float32)
    zunit = np.zeros((128, 2048), np.float32)

    def unit(mat, idx):
        if idx >= 44:
            return zunit
        if mat == 0:
            return w1[idx * 128:(idx + 1) * 128, :]
        if mat == 1:
            return w2[idx * 128:(idx + 1) * 128, :]
        return np.ascontiguousarray(
            w3[:, idx * 128:(idx + 1) * 128].reshape(16, 128, 128)
            .transpose(1, 0, 2).reshape(128, 2048))

    def unit_t(mat, idx):
        u = unit(mat, idx)
        return np.ascontiguousarray(
            u.reshape(128, 16, 128).transpose(2, 1, 0).reshape(128, 2048))

    in_maps = []
    for c in range(8):
        b, h = c >> 1, c & 1
        m = dict(common)
        m["xo"] = np.ascontiguousarray(x[b, h * 1024:(h + 1) * 1024, :])
        m["xa"] = (np.ascontiguousarray(x[b, 0:1024, :]) if h == 1 else zeros)
        m["ng"] = _ng_for(h)
        m["wsl"] = np.stack([unit_t(j // 6, 6 * c + (j % 6))
                             for j in range(18)])
        in_maps.append(m)
    return in_maps


def kernel(**inputs):
    nc = _get_nc(debug=False)
    in_maps = make_in_maps(inputs)
    res = run_bass_kernel_spmd(nc, in_maps, core_ids=list(range(8)))
    out = np.empty((B, S, H), np.float32)
    for c in range(8):
        b, h = c >> 1, c & 1
        out[b, h * 1024:(h + 1) * 1024, :] = res.results[c]["out"]
    return out


# version 30
# speedup vs baseline: 1.0395x; 1.0194x over previous
import numpy as np
import ml_dtypes
from contextlib import ExitStack

import concourse.bass as bass
import concourse.mybir as mybir
from concourse.bass_utils import run_bass_kernel_spmd

F32 = mybir.dt.float32
BF16 = mybir.dt.bfloat16
AX = mybir.AxisListType.X
OP = mybir.AluOpType
AF = mybir.ActivationFunctionType

B, S, H = 4, 2048, 2048
NH, KVH, D = 16, 4, 128
FF = 5632
NT_OWN = 1024
NC_F = FF // 128
EPS_LN = 1e-5
EPS_Q = 1e-8
QSCALE = 1.0 / float(np.sqrt(D))
NEG = -1.0e9
G3 = 8


class _Op:
    __slots__ = ("eng", "fn", "waits", "incs")


class Sched:
    def __init__(self):
        self.ops = []
        self.cnt = {}

    def tok_now(self, sem):
        return (sem, self.cnt.get(sem, 0))

    def op(self, eng, fn, after=(), tok=False, dma=None, inc=None):
        waits = [t for t in after if t is not None]
        incs = []
        t = None
        if dma is not None:
            self.cnt[dma] = self.cnt.get(dma, 0) + 16
            t = (dma, self.cnt[dma])
            incs.append((dma, 16))
        elif inc is not None:
            sem, amt = inc
            self.cnt[sem] = self.cnt.get(sem, 0) + amt
            t = (sem, self.cnt[sem])
            incs.append((sem, amt))
        elif tok:
            sem = "e_" + eng
            self.cnt[sem] = self.cnt.get(sem, 0) + 1
            t = (sem, self.cnt[sem])
            incs.append((sem, 1))
        o = _Op()
        o.eng, o.fn, o.waits, o.incs = eng, fn, waits, incs
        self.ops.append(o)
        return t

    def emit(self, eng_name, eng, sems):
        water = {}
        for o in self.ops:
            if o.eng != eng_name:
                continue
            for (s, v) in o.waits:
                if v <= 0 or water.get(s, 0) >= v:
                    continue
                eng.wait_ge(sems[s], v)
                water[s] = v
            ins = o.fn(eng)
            for (s, amt) in o.incs:
                ins.then_inc(sems[s], amt)


class Slot:

    def __init__(self, sched, sem):
        self.S = sched
        self.sem = sem
        self.free = []
        self.ready = None

    def load(self, eng, fns, after=()):
        t = None
        aft = list(self.free) + list(after)
        for fn in fns:
            t = self.S.op(eng, fn, after=aft, dma=self.sem)
            aft = []
        self.free = []
        self.ready = t
        return t

    def consumed(self, tok):
        if tok is not None:
            self.free.append(tok)


def build_nc(debug=False):
    nc = bass.Bass("TRN2", target_bir_lowering=False, debug=debug)

    d_xo = nc.dram_tensor("xo", [NT_OWN, H], F32, kind="ExternalInput").ap()
    d_xa = nc.dram_tensor("xa", [NT_OWN, H], F32, kind="ExternalInput").ap()
    d_wq = nc.dram_tensor("wq", [H, H], F32, kind="ExternalInput").ap()
    d_wk = nc.dram_tensor("wk", [H, KVH * D], F32, kind="ExternalInput").ap()
    d_wv = nc.dram_tensor("wv", [H, KVH * D], F32, kind="ExternalInput").ap()
    d_wo = nc.dram_tensor("wo", [H, H], F32, kind="ExternalInput").ap()
    d_wsl = nc.dram_tensor("wsl", [18, 128, 2048], F32, kind="ExternalInput").ap()
    d_gsl = nc.dram_tensor("gsl", [18, 1], F32, kind="Internal").ap()
    d_gfull = nc.dram_tensor("gfull", [144, 1], F32, kind="Internal").ap()
    d_qsl = nc.dram_tensor("qsl", [18, 128, 2048], BF16, kind="Internal").ap()
    d_qfull = nc.dram_tensor("qfull", [144, 128, 2048], BF16, kind="Internal",
                             addr_space="Shared").ap()
    d_maskw = nc.dram_tensor("maskw", [128, 640], BF16, kind="ExternalInput").ap()
    d_idb = nc.dram_tensor("identb", [128, 128], BF16, kind="ExternalInput").ap()
    d_idf = nc.dram_tensor("identf", [128, 128], F32, kind="ExternalInput").ap()
    d_ng = nc.dram_tensor("ng", [128, 8], F32, kind="ExternalInput").ap()
    d_out = nc.dram_tensor("out", [NT_OWN, H], F32, kind="ExternalOutput").ap()
    d_dbg = {}
    if debug:
        for nm, shape in [("dbg_xl1", [128, 16384]), ("dbg_k", [128, 8192]),
                          ("dbg_v", [128, 8192]), ("dbg_q", [128, 16384]),
                          ("dbg_av", [128, 16384]), ("dbg_xattn", [128, 16384]),
                          ("dbg_xl2", [128, 16384]), ("dbg_tau", [128, 8])]:
            d_dbg[nm] = nc.dram_tensor(nm, shape, F32, kind="ExternalOutput").ap()

    S = Sched()
    sem_names = ["e_tensor", "e_vector", "e_scalar", "e_gpsimd",
                 "f1a", "f1b", "f1c", "wpa", "wpb", "d_const", "d_cast0",
                 "d_cast1", "d_out0", "d_out1", "d_out2", "d_qst", "d_gst",
                 "d_qs0", "d_qs1", "d_qs2",
                 "d_gld", "cc1", "cc2", "qw1l", "qw2l", "wo0", "wo1", "wo2",
                 "d_cast2", "d_cast3"] + \
        [f"q3l{i}" for i in range(8)] + [f"d_dbg{i}" for i in range(8)]

    with ExitStack() as ctx:
        E = ctx.enter_context
        X = E(nc.sbuf_tensor("Xr", [128, 16384], F32))
        A1 = E(nc.sbuf_tensor("A1", [128, 16384], BF16))
        A2 = E(nc.sbuf_tensor("A2", [128, 16384], BF16))
        A3 = E(nc.sbuf_tensor("A3", [128, 8192], BF16))
        A4 = E(nc.sbuf_tensor("A4", [128, 8192], BF16))
        F1 = [E(nc.sbuf_tensor(f"F1{i}", [128, 2048], F32)) for i in range(3)]
        SC = E(nc.sbuf_tensor("SCb", [128, 4736], BF16))
        WQP = E(nc.sbuf_tensor("WQP", [128, 4096], BF16))
        ROW = E(nc.sbuf_tensor("ROWr", [1, 512], F32))
        MW = E(nc.sbuf_tensor("MW", [128, 640], BF16))
        IDB = E(nc.sbuf_tensor("IDB", [128, 128], BF16))
        IDF = E(nc.sbuf_tensor("IDF", [128, 128], F32))
        NGT = E(nc.sbuf_tensor("NGT", [128, 8], F32))
        ST = E(nc.sbuf_tensor("STt", [128, 64], F32))
        NM = E(nc.sbuf_tensor("NMt", [128, 16], F32))
        GA = E(nc.sbuf_tensor("GAt", [128, 144], F32))
        TA = E(nc.sbuf_tensor("TAt", [128, 8], F32))
        G1 = E(nc.sbuf_tensor("G1t", [1, 16], F32))
        G2 = E(nc.sbuf_tensor("G2t", [18, 1], F32))
        GF8 = E(nc.sbuf_tensor("GF8t", [8, 20], F32))
        CNEG = E(nc.sbuf_tensor("CNEG", [128, 1], F32))
        NGC = E(nc.sbuf_tensor("NGC", [128, 8], F32))
        ONESC = E(nc.sbuf_tensor("ONESC", [1, 128], F32))
        ONESPB = E(nc.sbuf_tensor("ONESPB", [128, 1], BF16))
        ONESPF = E(nc.sbuf_tensor("ONESPF", [128, 1], F32))

        PS_A = E(nc.psum_tensor("PS_A", [128, 1536], F32))
        PS_B = E(nc.psum_tensor("PS_B", [128, 1536], F32))
        PS_C = E(nc.psum_tensor("PS_C", [128, 512], F32))
        PS_D = E(nc.psum_tensor("PS_D", [128, 512], F32))

        sems = {n: E(nc.semaphore(n)) for n in sem_names}
        block = E(nc.Block())

        Xv, A1v, A2v, A3v, A4v = X[:], A1[:], A2[:], A3[:], A4[:]
        SCv, WQPv = SC[:], WQP[:]
        X3 = Xv.rearrange("p (c u) -> p c u", c=16)
        A13 = A1v.rearrange("p (c u) -> p c u", c=16)
        A23 = A2v.rearrange("p (c u) -> p c u", c=16)

        f1slots = [Slot(S, n) for n in ("f1a", "f1b", "f1c")]
        wpslots = [Slot(S, n) for n in ("wpa", "wpb")]

        dump_n = [0]

        def dump(name, src, after):
            if not debug:
                return None
            sem = f"d_dbg{dump_n[0]}"
            dump_n[0] += 1
            return S.op("gpsimd", lambda e, d=d_dbg[name], sv=src: e.dma_start(
                out=d, in_=sv), after=after, dma=sem)

        t_const = None
        for (dst, src) in [(MW[:], d_maskw), (IDB[:], d_idb), (IDF[:], d_idf),
                           (NGT[:], d_ng)]:
            t_const = S.op("sync", lambda e, d=dst, s=src: e.dma_start(out=d, in_=s),
                           dma="d_const")
        S.op("vector", lambda e: e.memset(ONESC[:], 1.0))
        S.op("vector", lambda e: e.memset(CNEG[:], -16.0))
        t_ngc = S.op("vector", lambda e: e.tensor_scalar(
            NGC[:], NGT[:], float(np.exp(-16.0)), None, OP.mult),
            after=[t_const], tok=True)
        t_ta0 = S.op("vector", lambda e: e.memset(TA[:], 0.0), tok=True)
        S.op("vector", lambda e: e.memset(ONESPB[:], 1.0))
        t_ones = S.op("vector", lambda e: e.memset(ONESPF[:], 1.0), tok=True)

        gamma_state = {"i": 0, "last": []}

        def emit_gamma(n):
            for _ in range(n):
                j = gamma_state["i"]
                if j >= 18:
                    return
                gamma_state["i"] = j + 1
                sidx = j % 3
                slot = f1slots[sidx]
                tl = slot.load("sync", [lambda e, sx=sidx, jj=j: e.dma_start(
                    out=F1[sx][:], in_=d_wsl[jj])])
                if j % 2 == 0:
                    ta = S.op("scalar", lambda e, sx=sidx, c=j: e.activation(
                        F1[sx][:], F1[sx][:], AF.Abs, accum_out=GA[:, c:c + 1]),
                        after=[tl], tok=True)
                else:
                    ta = S.op("vector", lambda e, sx=sidx, c=j: e.reduce_sum(
                        GA[:, c:c + 1], F1[sx][:], AX, apply_absolute_value=True),
                        after=[tl], tok=True)
                slot.consumed(ta)
                gamma_state["last"].append(ta)

        ps_rot = [PS_C, PS_D]
        ps_free = {id(PS_C): None, id(PS_D): None}
        sc_slot_free = [None, None]
        t_xl1_done = None
        t_xl1_half = [None, None]
        t_xfm_done = None
        prev_chain = [None, None]
        for t in range(16):
            own = t >= 8
            slot = f1slots[t % 3]
            sidx = t % 3
            src = d_xo if own else d_xa
            r0 = ((t - 8) if own else t) * 128
            tload = slot.load("sync", [lambda e, s=sidx, ss=src, r=r0: e.dma_start(
                out=F1[s][:], in_=ss[r:r + 128, :])])
            ftile = F1[sidx][:]
            t_red = S.op("vector", lambda e, ft=ftile, c=t: e.reduce_sum(
                ST[:, c:c + 1], ft, AX), after=[tload], tok=True)
            snk = SCv[:, (t % 2) * 2048:(t % 2) * 2048 + 2048]
            nb0 = 6 * (t % 2)
            t_sq = S.op("scalar", lambda e, ft=ftile, c=t, sk=snk: e.activation(
                sk, ft, AF.Square, accum_out=ST[:, 16 + c:17 + c]),
                after=[tload, sc_slot_free[t % 2]], tok=True)
            t_c = S.op("vector", lambda e, c=t: e.tensor_scalar(
                ST[:, 32 + c:33 + c], ST[:, c:c + 1], 1.0 / H, None, OP.mult),
                after=[t_red, prev_chain[t % 2]], tok=True)
            t_c = S.op("vector", lambda e, c=t, n0=nb0: e.tensor_mul(
                NM[:, n0:n0 + 1], ST[:, 32 + c:33 + c], ST[:, 32 + c:33 + c]),
                after=[t_c], tok=True)
            t_c = S.op("vector", lambda e, c=t, n0=nb0: e.tensor_scalar(
                NM[:, n0 + 1:n0 + 2], ST[:, 16 + c:17 + c], 1.0 / H, None, OP.mult),
                after=[t_sq, t_c], tok=True)
            t_c = S.op("vector", lambda e, n0=nb0: e.tensor_sub(
                NM[:, n0 + 2:n0 + 3], NM[:, n0 + 1:n0 + 2], NM[:, n0:n0 + 1]),
                after=[t_c], tok=True)
            t_c = S.op("vector", lambda e, n0=nb0: e.tensor_scalar(
                NM[:, n0 + 2:n0 + 3], NM[:, n0 + 2:n0 + 3], EPS_LN, None, OP.add),
                after=[t_c], tok=True)
            t_c = S.op("vector", lambda e, n0=nb0: e.reciprocal(
                NM[:, n0 + 3:n0 + 4], NM[:, n0 + 2:n0 + 3]), after=[t_c], tok=True)
            t_s = S.op("scalar", lambda e, n0=nb0: e.activation(
                NM[:, n0 + 4:n0 + 5], NM[:, n0 + 3:n0 + 4], AF.Sqrt),
                after=[t_c], tok=True)
            t_c = S.op("vector", lambda e, n0=nb0: e.tensor_mul(
                NM[:, n0 + 5:n0 + 6], NM[:, n0 + 4:n0 + 5], NM[:, n0 + 4:n0 + 5]),
                after=[t_s], tok=True)
            t_c = S.op("vector", lambda e, n0=nb0: e.tensor_mul(
                NM[:, n0 + 5:n0 + 6], NM[:, n0 + 5:n0 + 6], NM[:, n0 + 2:n0 + 3]),
                after=[t_c], tok=True)
            t_c = S.op("vector", lambda e, n0=nb0: e.tensor_scalar(
                NM[:, n0 + 5:n0 + 6], NM[:, n0 + 5:n0 + 6], 3.0, -0.5,
                OP.subtract, OP.mult), after=[t_c], tok=True)
            t_c = S.op("vector", lambda e, c=t, n0=nb0: e.tensor_mul(
                ST[:, 48 + c:49 + c], NM[:, n0 + 4:n0 + 5], NM[:, n0 + 5:n0 + 6]),
                after=[t_c], tok=True)
            prev_chain[t % 2] = t_c
            xt_tm = snk
            t_norm = S.op("vector", lambda e, ft=ftile, c=t, dv=xt_tm: e.tensor_scalar(
                dv, ft, ST[:, 32 + c:33 + c], ST[:, 48 + c:49 + c],
                OP.subtract, OP.mult), after=[t_c, t_sq], tok=True)
            dst3 = A13 if own else A23
            tt_own = (t - 8) if own else t
            for r in range(4):
                ps = ps_rot[r % 2]
                trs = None
                for cc in range(4):
                    dc = 4 * r + cc
                    trs = S.op("tensor", lambda e, p=ps, d=dc, c=cc, sv=xt_tm:
                               e.transpose(
                                   p[:].bitcast(BF16)[:, c * 128:(c + 1) * 128],
                                   sv[:, d * 128:(d + 1) * 128], IDB[:]),
                               after=([t_norm, t_const, ps_free[id(ps)]]
                                      if cc == 0 else []), tok=(cc == 3))
                if r % 2 == 0:
                    ev = S.op("vector", lambda e, p=ps, rr=r, d3=dst3, tt=tt_own:
                              e.tensor_copy(
                                  d3[:, 4 * rr:4 * rr + 4, tt * 128:(tt + 1) * 128],
                                  p[:].bitcast(BF16)[:, 0:512].rearrange(
                                      "p (c u) -> p c u", c=4)),
                              after=[trs], tok=True)
                else:
                    ev = S.op("scalar", lambda e, p=ps, rr=r, d3=dst3, tt=tt_own:
                              e.activation(
                                  d3[:, 4 * rr:4 * rr + 4, tt * 128:(tt + 1) * 128],
                                  p[:].bitcast(BF16)[:, 0:512].rearrange(
                                      "p (c u) -> p c u", c=4), AF.Copy),
                              after=[trs], tok=True)
                ps_free[id(ps)] = ev
                if r % 2 == 0:
                    ev_d_ = ev
                else:
                    ev_a_ = ev
            t_xl1_done = [ev_d_, ev_a_]
            t_xl1_half[1 if own else 0] = t_xl1_done
            sc_slot_free[t % 2] = S.tok_now("e_tensor")
            if own:
                for r in range(4):
                    ps = ps_rot[r % 2]
                    trs = None
                    for cc in range(4):
                        dc = 4 * r + cc
                        trs = S.op("tensor", lambda e, p=ps, d=dc, c=cc, ft=ftile:
                                   e.transpose(p[:, c * 128:(c + 1) * 128],
                                               ft[:, d * 128:(d + 1) * 128], IDF[:]),
                                   after=([tload, t_const, ps_free[id(ps)]]
                                          if cc == 0 else []), tok=(cc == 3))
                    if r % 2 == 0:
                        ev = S.op("vector", lambda e, p=ps, rr=r, tt=tt_own:
                                  e.tensor_copy(
                                      X3[:, 4 * rr:4 * rr + 4,
                                         tt * 128:(tt + 1) * 128],
                                      p[:, 0:512].rearrange("p (c u) -> p c u",
                                                            c=4)),
                                  after=[trs], tok=True)
                    else:
                        ev = S.op("scalar", lambda e, p=ps, rr=r, tt=tt_own:
                                  e.activation(
                                      X3[:, 4 * rr:4 * rr + 4,
                                         tt * 128:(tt + 1) * 128],
                                      p[:, 0:512].rearrange("p (c u) -> p c u",
                                                            c=4), AF.Copy),
                                  after=[trs], tok=True)
                    ps_free[id(ps)] = ev
                    if r % 2 == 0:
                        evx_d_ = ev
                    else:
                        evx_a_ = ev
                t_xfm_done = [evx_d_, evx_a_]
            slot.consumed(t_norm)
            if own:
                slot.consumed(S.tok_now("e_tensor"))
            if t >= 2:
                emit_gamma(2)

        t_dbg_xl1 = dump("dbg_xl1", A1v, list(t_xl1_done))


        def panel_load(slot_i, dw, ocol, after=()):
            slot = wpslots[slot_i]
            dst = WQPv[:, slot_i * 2048:(slot_i + 1) * 2048]
            dst3 = dst.rearrange("p (c o) -> p c o", c=16)
            fns = []
            for cc in range(4):
                sr = dw[cc * 512:(cc + 1) * 512, ocol:ocol + 128].rearrange(
                    "(c p) o -> p c o", p=128)
                fns.append(lambda e, s3=dst3, c4=cc, s=sr: e.dma_start(
                    out=s3[:, c4 * 4:(c4 + 1) * 4, :], in_=s))
            t = slot.load("gpsimd", fns, after=after)
            return slot, dst, t

        t_kv_done = None
        p1_panels = [(wi_, pj_) for wi_ in range(2) for pj_ in range(4)]
        p1_pre = [panel_load(i_ % 2, [d_wk, d_wv][p[0]], p[1] * 128)
                  for i_, p in enumerate(p1_panels[:2])]
        emit_gamma(18)
        ga_last = gamma_state["last"]
        ga_act = [t for t in ga_last if t[0] == "e_scalar"]
        ga_dve = [t for t in ga_last if t[0] == "e_vector"]
        t_glast = ([ga_act[-1]] if ga_act else []) + ([ga_dve[-1]] if ga_dve else [])
        t_gmm = S.op("tensor", lambda e: e.matmul(
            PS_B[0:18, 0:1], GA[:, 0:18], ONESPF[:], start=True, stop=True),
            after=t_glast + [t_ones], tok=True)
        t_gcp = S.op("vector", lambda e: e.tensor_copy(G2[:], PS_B[0:18, 0:1]),
                     after=[t_gmm], tok=True)
        t_gst = S.op("sync", lambda e: e.dma_start(out=d_gsl, in_=G2[:]),
                     after=[t_gcp], dma="d_gst")
        t_cc1 = S.op("gpsimd", lambda e: e.collective_compute(
            "AllGather", OP.bypass, replica_groups=[list(range(8))],
            ins=[d_gsl.opt()], outs=[d_gfull.opt()]),
            after=[t_gst], inc=("cc1", 1))
        t_gld = S.op("sync", lambda e: e.dma_start(
            out=GF8[0:8, 0:18], in_=d_gfull.rearrange("(c j) o -> c (j o)", j=18)),
            after=[t_cc1], dma="d_gld")
        t_tau = []
        prev_tau_tok = None
        for mi in range(3):
            tg = S.op("vector", lambda e, m=mi: e.reduce_sum(
                GF8[0:8, 19:20], GF8[0:8, 6 * m:6 * m + 6],
                AX), after=[t_gld, prev_tau_tok], tok=True)
            tmm = S.op("tensor", lambda e, m=mi: e.matmul(
                PS_B[0:1, 500 + m:501 + m], GF8[0:8, 19:20], ONESPF[0:8, 0:1],
                start=True, stop=True), after=[tg], tok=True)
            tc = S.op("vector", lambda e, m=mi: e.tensor_scalar(
                G1[0:1, m:m + 1], PS_B[0:1, 500 + m:501 + m],
                1.0 / (FF * H), EPS_Q, OP.mult, OP.add), after=[tmm], tok=True)
            tc2 = S.op("vector", lambda e, m=mi: e.tensor_scalar(
                G1[0:1, 4 + m:5 + m], G1[0:1, m:m + 1], 0.5, None, OP.mult),
                after=[tc], tok=True)
            tc3 = S.op("vector", lambda e, m=mi: e.tensor_scalar(
                G1[0:1, 8 + m:9 + m], G1[0:1, m:m + 1], -0.5, None, OP.mult),
                after=[tc2], tok=True)
            tb = S.op("tensor", lambda e, m=mi: e.matmul(
                PS_B[:, 504 + m:505 + m], ONESC[:], G1[0:1, 4 + m:5 + m],
                start=True, stop=True), after=[tc3], tok=True)
            tb2 = S.op("tensor", lambda e, m=mi: e.matmul(
                PS_B[:, 508 + m:509 + m], ONESC[:], G1[0:1, 8 + m:9 + m],
                start=True, stop=True), after=[tb], tok=True)
            tv = S.op("vector", lambda e, m=mi: e.tensor_copy(
                TA[:, m:m + 1], PS_B[:, 504 + m:505 + m]), after=[tb2], tok=True)
            tv = S.op("vector", lambda e, m=mi: e.tensor_copy(
                TA[:, 4 + m:5 + m], PS_B[:, 508 + m:509 + m]), after=[tv], tok=True)
            t_tau.append(tv)
            prev_tau_tok = tv
        t_taud = t_tau[-1]

        q_state = {"i": 0, "prevadd": t_taud,
                   "sthome": [t_taud, t_taud], "pend": None,
                   "stl": [], "ccs": []}

        def _emit_pend():
            pe = q_state["pend"]
            if pe is None:
                return
            q_state["pend"] = None
            jj_, av_, sx_, tadd_ = pe
            tst = S.op("scalar", lambda e, jj=jj_, av=av_: e.dma_start(
                out=d_qsl[jj], in_=av), after=[tadd_], dma=f"d_qs{sx_}")
            q_state["sthome"][jj_ % 2] = tst
            q_state["stl"].append((jj_, tst))
            if jj_ in (5, 11):
                m3 = jj_ // 6
                in_m = d_qsl[6 * m3:6 * m3 + 6].opt()
                out_m = d_qfull[48 * m3:48 * m3 + 48].opt()
                aft_m = [tok for (j2, tok) in q_state["stl"]
                         if 6 * m3 <= j2 < 6 * m3 + 6][-3:]
                q_state["ccs"].append(S.op(
                    "gpsimd", lambda e, i_=in_m, o_=out_m:
                    e.collective_compute(
                        "AllGather", OP.bypass,
                        replica_groups=[list(range(8))],
                        ins=[i_], outs=[o_]),
                    after=aft_m, inc=("cc2", 1)))

        def emit_quant(n):
            for _ in range(n):
                j = q_state["i"]
                if j >= 18:
                    return
                q_state["i"] = j + 1
                m = j // 6
                sidx = j % 3
                slot = f1slots[sidx]
                tl = slot.load("sync", [lambda e, sx=sidx, jj=j: e.dma_start(
                    out=F1[sx][:], in_=d_wsl[jj])])
                _emit_pend()
                h = j % 2
                a_ = SCv[:, h * 2048:h * 2048 + 2048]
                b_ = SCv[:, (1 - h) * 2048:(1 - h) * 2048 + 2048]
                tsa = S.op("scalar", lambda e, sx=sidx, mm_=m, av=a_:
                           e.activation(
                               av, F1[sx][:], AF.Sign,
                               bias=TA[:, 4 + mm_:5 + mm_]),
                           after=[tl, t_tau[m], q_state["prevadd"],
                                  q_state["sthome"][h]], tok=True)
                tsb = S.op("scalar", lambda e, sx=sidx, mm_=m, bv=b_:
                           e.activation(
                               bv, F1[sx][:], AF.Sign, bias=TA[:, mm_:mm_ + 1]),
                           after=[tsa, q_state["sthome"][1 - h]], tok=True)
                slot.consumed(tsb)
                tadd = S.op("vector", lambda e, av=a_, bv=b_: e.tensor_tensor(
                    av, av, bv, OP.add), after=[tsb], tok=True)
                q_state["prevadd"] = tadd
                q_state["pend"] = (j, a_, sidx, tadd)

        for p1i in range(8):
            wi, pj = p1_panels[p1i]
            if True:
                slot, pan, t_pan = p1_pre[p1i]
                t_last_mm = None
                for tc in range(4):
                    src_ar = A2v if tc < 2 else A1v
                    th = tc % 2
                    ps = ps_rot[tc % 2]
                    mm = None
                    for dc in range(16):
                        mm = S.op("tensor", lambda e, p=ps, pn=pan, d=dc, sa=src_ar,
                                  t2=th: e.matmul(
                                      p[:], pn[:, d * 128:(d + 1) * 128],
                                      sa[:, d * 1024 + t2 * 512:
                                         d * 1024 + t2 * 512 + 512],
                                      start=(d == 0), stop=(d == 15)),
                                  after=(([t_pan, ps_free[id(ps)]] +
                                          t_xl1_half[0 if tc < 2 else 1])
                                         if dc == 0 else []), tok=(dc == 15))
                    t_last_mm = mm
                    dst = (A3v if wi == 0 else A4v)[
                        :, pj * 2048 + tc * 512: pj * 2048 + tc * 512 + 512]
                    ev = S.op("vector", lambda e, p=ps, d=dst: e.tensor_copy(
                        d, p[:]), after=[mm], tok=True)
                    ps_free[id(ps)] = ev
                    t_kv_done = ev
                slot.consumed(t_last_mm)
                if p1i + 2 < 8:
                    pn_ = p1_panels[p1i + 2]
                    p1_pre.append(panel_load((p1i + 2) % 2,
                                             [d_wk, d_wv][pn_[0]],
                                             pn_[1] * 128))
                emit_quant(1)
        t_vtm_all = None
        for kvh in range(KVH):
            for kt in range(16):
                base = kvh * 2048 + kt * 128
                ps = ps_rot[kt % 2]
                tr = S.op("tensor", lambda e, p=ps, b=base: e.transpose(
                    p[:].bitcast(BF16)[:, 0:128], A4v[:, b:b + 128], IDB[:]),
                    after=[t_kv_done, ps_free[id(ps)]], tok=True)
                ev = S.op("vector", lambda e, p=ps, b=base: e.tensor_copy(
                    A4v[:, b:b + 128], p[:].bitcast(BF16)[:, 0:128]),
                    after=[tr], tok=True)
                ps_free[id(ps)] = ev
                t_vtm_all = ev

        t_dbg_k = dump("dbg_k", A3v, [t_kv_done])
        t_dbg_v = dump("dbg_v", A4v, [t_vtm_all])

        t_q_done = None
        for pj in range(16):
            slot, pan, t_pan = panel_load(pj % 2, d_wq, pj * 128)
            t_last_mm = None
            for th in range(2):
                ps = ps_rot[th % 2]
                mm = None
                for dc in range(16):
                    mm = S.op("tensor", lambda e, p=ps, pn=pan, d=dc, t2=th:
                              e.matmul(p[:], pn[:, d * 128:(d + 1) * 128],
                                       A1v[:, d * 1024 + t2 * 512:
                                           d * 1024 + t2 * 512 + 512],
                                       start=(d == 0), stop=(d == 15)),
                              after=([t_pan, t_kv_done, ps_free[id(ps)]]
                                     if dc == 0 else []), tok=(dc == 15))
                t_last_mm = mm
                dst = A2v[:, pj * 1024 + th * 512: pj * 1024 + th * 512 + 512]
                ev = S.op("vector", lambda e, p=ps, d=dst: e.tensor_scalar(
                    d, p[:], QSCALE, None, OP.mult), after=[mm], tok=True)
                ps_free[id(ps)] = ev
                t_q_done = ev
            slot.consumed(t_last_mm)
            emit_quant(1)

        t_dbg_q = dump("dbg_q", A2v, [t_q_done])

        emit_quant(18)
        _emit_pend()
        in_m = d_qsl[12:18].opt()
        out_m = d_qfull[96:144].opt()
        aft_m = [tok for (jj2, tok) in q_state["stl"] if 12 <= jj2 < 18][-3:]
        t_cc2 = S.op("gpsimd", lambda e, i_=in_m, o_=out_m:
                     e.collective_compute(
                         "AllGather", OP.bypass,
                         replica_groups=[list(range(8))],
                         ins=[i_], outs=[o_]),
                     after=aft_m, inc=("cc2", 1))
        t_q_sc0, t_q_sc1 = q_state["sthome"]

        sc_rot = [PS_A, PS_B]
        sc_free = {id(PS_A): [t_taud], id(PS_B): [t_taud]}
        avst = {"psfree": ps_free[id(PS_C)], "mm": None, "all": None}
        p_free = [t_q_done, t_q_done]
        nm_free = [None, None]
        pending_av = []
        oproj_jobs = []
        oproj_state = {"load_i": 0, "mm_i": 0, "done": None, "ocd": {}}
        wo_slots = [Slot(S, f"wo{i}") for i in range(3)]
        for sl_ in wo_slots:
            sl_.consumed(t_q_done)
        wov = [WQPv[:, 2560 + i * 512:2560 + (i + 1) * 512] for i in range(3)]
        opps_free = {id(PS_C): None, id(PS_D): None}

        def pump_oproj(n):
            for _ in range(n):
                li = oproj_state["load_i"]
                if li < len(oproj_jobs) and li - oproj_state["mm_i"] < 5:
                    kv_, oc_, th_, _avtok = oproj_jobs[li]
                    if th_ == 0:
                        sl = wo_slots[(li // 2) % 3]
                        src = d_wo[(4 * kv_) * 128:(4 * kv_ + 4) * 128,
                                   oc_ * 128:(oc_ + 1) * 128].rearrange(
                                       "(c p) o -> p c o", p=128)
                        dst3 = wov[(li // 2) % 3].rearrange(
                            "p (c o) -> p c o", c=4)
                        sl.load("gpsimd", [lambda e, d3=dst3, s_=src:
                                           e.dma_start(out=d3, in_=s_)])
                    oproj_state["load_i"] = li + 2 if th_ == 0 else li + 1
                mi = oproj_state["mm_i"]
                if mi >= len(oproj_jobs) or mi >= oproj_state["load_i"]:
                    continue
                kv_, oc_, th_, avtok = oproj_jobs[mi]
                oproj_state["mm_i"] = mi + 1
                sl = wo_slots[(mi // 2) % 3]
                pan = wov[(mi // 2) % 3]
                ps = ps_rot[mi % 2]
                mm = None
                for cc in range(4):
                    mm = S.op("tensor", lambda e, p=ps, pn=pan, c=cc, kv2=kv_,
                              t2=th_: e.matmul(
                                  p[:], pn[:, c * 128:(c + 1) * 128],
                                  A1v[:, (4 * kv2 + c) * 1024 + t2 * 512:
                                      (4 * kv2 + c) * 1024 + t2 * 512 + 512],
                                  start=(c == 0), stop=(c == 3)),
                              after=(([sl.ready, avtok, opps_free[id(ps)],
                                       av_free[id(ps)]] + list(t_xfm_done))
                                     if cc == 0 else []),
                              tok=(cc == 3))
                if th_ == 1:
                    sl.consumed(mm)
                xsl = Xv[:, oc_ * 1024 + th_ * 512:oc_ * 1024 + th_ * 512 + 512]
                ev2 = S.op("vector", lambda e, p=ps, xs=xsl: e.tensor_tensor(
                    xs, p[:], xs, OP.add),
                    after=[mm, oproj_state["xw"]], tok=True)
                opps_free[id(ps)] = ev2
                oproj_state["xw"] = ev2
                oproj_state["done"] = ev2
                if kv_ == 3:
                    oproj_state["ocd"][(oc_, th_)] = ev2
        oproj_state["xw"] = None
        pending_trans = [None]
        pT_toks = []
        pT_prev = []

        av_free = {id(PS_C): ps_free[id(PS_C)],
                   id(PS_D): ps_free[id(PS_D)]}

        def emit_av(kv, q, my_toks):
            psd = [PS_C, PS_D][q % 2]
            mm = None
            for w in range(9):
                kt = q + w
                mm = S.op("tensor", lambda e, p=psd, kv_=kv, k=kt, ww=w:
                          e.matmul(
                              p[:, 0:512],
                              A4v[:, (kv_ * 16 + k) * 128:
                                  (kv_ * 16 + k) * 128 + 128],
                              SCv[:, ww * 512:ww * 512 + 512],
                              start=(ww == 0), stop=(ww == 8)),
                          after=(my_toks + [t_vtm_all, av_free[id(psd)],
                                            opps_free[id(psd)]]
                                 if w == 0 else []),
                          tok=(w == 8))
            ev = S.op("scalar", lambda e, kv_=kv, q_=q, p=psd: e.activation(
                A1v[:, kv_ * 4096:(kv_ + 1) * 4096].rearrange(
                    "p (c u) -> p c u", c=4)[:, :, q_ * 128:(q_ + 1) * 128],
                p[:, 0:512].rearrange("p (c u) -> p c u", c=4), AF.Copy),
                after=[mm, t_dbg_xl1], tok=True)
            av_free[id(psd)] = ev
            avst["psfree"] = ev
            avst["mm"] = mm
            avst["all"] = ev
            if q == 7:
                for oc_ in range(16):
                    for th_ in range(2):
                        oproj_jobs.append((kv, oc_, th_, ev))

        for kvh in range(KVH):
            for qt in range(8):
                for qh2 in range(4):
                    qh = kvh * 4 + qh2
                    par = qh2 % 2
                    nb = 8 * par
                    scp = sc_rot[par]
                    pbuf = WQPv[:, par * 1152:par * 1152 + 1152]
                    kbase = kvh * 2048 + qt * 128
                    qcol = qh * 1024 + qt * 128
                    sf = sc_free[id(scp)]
                    sf = sf if isinstance(sf, list) else [sf]
                    S.op("tensor", lambda e, p=scp: e.matmul(
                        p[:, 0:128], IDB[:], MW[:, 0:128], start=True, stop=False),
                        after=[t_q_done, t_vtm_all, t_const] + sf)
                    S.op("tensor", lambda e, p=scp: e.matmul(
                        p[:, 1024:1152], IDB[:], MW[:, 512:640],
                        start=True, stop=False))
                    S.op("tensor", lambda e, p=scp, q=qcol, k=kbase: e.matmul(
                        p[:, 0:128], A2v[:, q:q + 128], A3v[:, k:k + 128],
                        start=False, stop=True))
                    S.op("tensor", lambda e, p=scp, q=qcol, k=kbase: e.matmul(
                        p[:, 128:512], A2v[:, q:q + 128], A3v[:, k + 128:k + 512],
                        start=True, stop=True))
                    S.op("tensor", lambda e, p=scp, q=qcol, k=kbase: e.matmul(
                        p[:, 512:1024], A2v[:, q:q + 128], A3v[:, k + 512:k + 1024],
                        start=True, stop=True))
                    t_sc = S.op("tensor", lambda e, p=scp, q=qcol, k=kbase: e.matmul(
                        p[:, 1024:1152], A2v[:, q:q + 128], A3v[:, k + 1024:k + 1152],
                        start=False, stop=True), tok=True)
                    t_exp = S.op("scalar", lambda e, p=scp, pb=pbuf, n=nb:
                                 e.activation(pb, p[:, 0:1152], AF.Exp,
                                              bias=CNEG[:],
                                              accum_out=NM[:, n:n + 1]),
                                 after=[t_sc, p_free[par], nm_free[par]], tok=True)
                    if pending_trans[0] is not None:
                        pT_prev.append(pending_trans[0]())
                        pending_trans[0] = None
                    if qh2 == 0:
                        for fn_ in pending_av:
                            fn_()
                        pending_av = []
                        pT_toks = pT_prev
                        pT_prev = []
                    t_z = S.op("vector", lambda e, q=qt, n=nb: e.tensor_tensor(
                        NM[:, n + 1:n + 2], NM[:, n:n + 1], NGC[:, q:q + 1],
                        OP.subtract), after=[t_exp, t_ngc], tok=True)
                    t_z = S.op("vector", lambda e, n=nb: e.reciprocal(
                        NM[:, n + 2:n + 3], NM[:, n + 1:n + 2]), after=[t_z],
                        tok=True)
                    t_nrm = S.op("vector", lambda e, pb=pbuf, n=nb: e.tensor_scalar(
                        pb, pb, NM[:, n + 2:n + 3], None, OP.mult),
                        after=[t_z], tok=True)
                    nm_free[par] = t_nrm

                    def mk_trans(scp_=scp, pbuf_=pbuf, par_=par, qh2_=qh2,
                                 t_nrm_=t_nrm, t_exp_=t_exp, kvh_=kvh, qt_=qt):
                        def emit_trans():
                            trs = None
                            for w in range(9):
                                trs = S.op("tensor", lambda e, p=scp_, pb=pbuf_,
                                           ww=w: e.transpose(
                                               p[:].bitcast(BF16)
                                               [:, ww * 128:(ww + 1) * 128],
                                               pb[:, ww * 128:(ww + 1) * 128],
                                               IDB[:]),
                                           after=([t_nrm_, t_exp_] if w == 0
                                                  else []), tok=(w == 8))
                            p3g = ([t_q_sc0, t_q_sc1]
                                   if (kvh_ == 0 and qt_ == 0 and qh2_ == 0)
                                   else [])
                            ev_b = S.op("vector", lambda e, p=scp_, q2=qh2_:
                                        e.tensor_copy(
                                            SCv[:, 0:4608].rearrange(
                                                "p (w u) -> p w u", w=9)
                                            [:, :, q2 * 128:(q2 + 1) * 128],
                                            p[:].bitcast(BF16)[:, 0:1152]
                                            .rearrange("p (w u) -> p w u",
                                                       w=9)),
                                        after=[trs, avst["mm"]] + p3g, tok=True)
                            sc_free[id(scp_)] = [ev_b]
                            p_free[par_] = trs
                            return ev_b
                        return emit_trans

                    pending_trans[0] = mk_trans()
                    if qh2 >= 2:
                        pump_oproj(2)
                pending_av.append(
                    (lambda kv=kvh, q=qt: emit_av(kv, q, list(pT_toks) +
                                                  list(pT_prev) +
                                                  [S.tok_now("e_vector")])))
        if pending_trans[0] is not None:
            pT_prev.append(pending_trans[0]())
            pending_trans[0] = None
        pT_toks = pT_toks + pT_prev
        for fn_ in pending_av:
            fn_()
        pending_av = []
        t_av_all = avst["all"]
        av_mm_prev = avst["mm"]
        for i_ in range(2):
            wpslots[i_].consumed(av_mm_prev)

        ocd = oproj_state["ocd"]
        cast_free = {(h_, c_): t_av_all for h_ in range(2) for c_ in range(2)}
        mmx_h = [None, None]
        mmq_h = [None, None]
        sc_a = sc_free[id(PS_A)]
        sc_b = sc_free[id(PS_B)]
        sc_a = sc_a if isinstance(sc_a, list) else [sc_a]
        sc_b = sc_b if isinstance(sc_b, list) else [sc_b]
        for dc in range(16):
            pump_oproj(2)
            for hh2 in range(2):
                cslot = dc % 2
                xb = SCv[:, hh2 * 1024 + cslot * 512:
                         hh2 * 1024 + cslot * 512 + 512]
                xb2 = SCv[:, 2048 + hh2 * 1024 + cslot * 512:
                          2048 + hh2 * 1024 + cslot * 512 + 512]
                tcast = S.op("gpsimd", lambda e, d=dc, h2=hh2, b=xb: e.dma_start(
                    out=b,
                    in_=Xv[:, d * 1024 + h2 * 512:d * 1024 + h2 * 512 + 512]),
                    after=[ocd[(dc, hh2)], cast_free[(hh2, cslot)]],
                    dma=f"d_cast{2 * hh2 + cslot}")
                tsq = S.op("scalar", lambda e, b=xb, b2=xb2: e.activation(
                    b2, b, AF.Square),
                    after=[tcast, cast_free[(hh2, cslot)]], tok=True)
                mmx_h[hh2] = S.op("tensor", lambda e, b=xb, d=dc, h2=hh2:
                                  e.matmul(
                                      PS_A[0:1, h2 * 512:h2 * 512 + 512],
                                      ONESPB[:], b,
                                      start=(d == 0), stop=(d == 15)),
                                  after=([tcast, t_ones] +
                                         (sc_a if dc == 0 else [])),
                                  tok=True)
                mmq_h[hh2] = S.op("tensor", lambda e, b2=xb2, d=dc, h2=hh2:
                                  e.matmul(
                                      PS_B[0:1, h2 * 512:h2 * 512 + 512],
                                      ONESPB[:], b2,
                                      start=(d == 0), stop=(d == 15)),
                                  after=([tsq] + (sc_b if dc == 0 else [])),
                                  tok=True)
                cast_free[(hh2, cslot)] = mmq_h[hh2]
        pump_oproj(8)
        t_oproj_done = oproj_state["done"]
        x_write_tok = oproj_state["xw"]
        ps_free[id(PS_C)] = opps_free[id(PS_C)]
        ps_free[id(PS_D)] = opps_free[id(PS_D)]

        t_dbg_av = dump("dbg_av", A1v, [t_av_all])
        t_dbg_xattn = dump("dbg_xattn", Xv, [t_oproj_done])

        t_xl2_done = None
        xl2_half = [None, None]
        stats_f1_guard = list(f1slots[1].free)
        f1slots[1].free = []
        for hh in range(2):
            hs = hh * 512
            m_sb = F1[1][0:1, 0:512]
            q2_sb = F1[1][0:1, 512:1024]
            t0_sb = F1[1][0:1, 1024:1536]
            a_sb = F1[1][0:1, 1536:2048]
            u_sb = F1[2][0:1, 512:1024]
            r0_sb = F1[2][0:1, 1024:1536]
            tc = S.op("vector", lambda e, h2=hh: e.tensor_scalar(
                m_sb, PS_A[0:1, h2 * 512:h2 * 512 + 512], 1.0 / H, None,
                OP.mult),
                after=[mmx_h[hh], mmq_h[hh], t_xl2_done] + stats_f1_guard,
                tok=True)
            tc = S.op("vector", lambda e, h2=hh: e.tensor_scalar(
                q2_sb, PS_B[0:1, h2 * 512:h2 * 512 + 512], 1.0 / H, None,
                OP.mult), after=[tc], tok=True)
            tc = S.op("vector", lambda e: e.tensor_mul(
                t0_sb, m_sb, m_sb), after=[tc], tok=True)
            tc = S.op("vector", lambda e: e.tensor_sub(
                a_sb, q2_sb, t0_sb), after=[tc], tok=True)
            tc = S.op("vector", lambda e: e.tensor_scalar(
                a_sb, a_sb, EPS_LN, None, OP.add), after=[tc], tok=True)
            tc = S.op("vector", lambda e: e.reciprocal(
                u_sb, a_sb), after=[tc], tok=True)
            ts_ = S.op("scalar", lambda e: e.activation(
                r0_sb, u_sb, AF.Sqrt), after=[tc], tok=True)
            tc = S.op("vector", lambda e: e.tensor_mul(
                t0_sb, r0_sb, r0_sb), after=[ts_], tok=True)
            tc = S.op("vector", lambda e: e.tensor_mul(
                t0_sb, t0_sb, a_sb), after=[tc], tok=True)
            tc = S.op("vector", lambda e: e.tensor_scalar(
                t0_sb, t0_sb, 3.0, -0.5, OP.subtract, OP.mult),
                after=[tc], tok=True)
            tA = S.op("vector", lambda e: e.tensor_mul(
                ROW[0:1, 0:512], r0_sb, t0_sb), after=[tc], tok=True)
            tbA = S.op("tensor", lambda e, h2=hh: e.matmul(
                PS_B[:, h2 * 512:h2 * 512 + 512], ONESC[:], ROW[0:1, 0:512],
                start=True, stop=True), after=[tA], tok=True)
            tB = S.op("vector", lambda e: e.tensor_mul(
                t0_sb, m_sb, ROW[0:1, 0:512]), after=[tbA], tok=True)
            tB = S.op("vector", lambda e: e.tensor_scalar(
                ROW[0:1, 0:512], t0_sb, -1.0, None, OP.mult),
                after=[tB], tok=True)
            tbB = S.op("tensor", lambda e, h2=hh: e.matmul(
                PS_A[:, h2 * 512:h2 * 512 + 512], ONESC[:], ROW[0:1, 0:512],
                start=True, stop=True), after=[tB], tok=True)
            stats_f1_guard = []
            t_n = None
            for dc in range(16):
                tmp = F1[2][:]
                t1 = S.op("vector", lambda e, d=dc, h2=hh, tp=tmp:
                          e.tensor_tensor(
                              tp[:, 0:512],
                              Xv[:, d * 1024 + h2 * 512:
                                 d * 1024 + h2 * 512 + 512],
                              PS_B[:, h2 * 512:h2 * 512 + 512], OP.mult),
                          after=([tbB, t_n] +
                                 (list(f1slots[2].free) if dc == 0 else [])),
                          tok=True)
                t_n = S.op("vector", lambda e, d=dc, h2=hh, tp=tmp:
                           e.tensor_tensor(
                               A2v[:, d * 1024 + h2 * 512:
                                   d * 1024 + h2 * 512 + 512],
                               tp[:, 0:512],
                               PS_A[:, h2 * 512:h2 * 512 + 512], OP.add),
                           after=[t1, t_dbg_q], tok=True)
            if hh == 0:
                f1slots[2].free = []
            t_xl2_done = t_n
            xl2_half[hh] = t_n
        f1slots[2].consumed(t_xl2_done)
        f1slots[1].consumed(t_xl2_done)

        t_dbg_xl2 = dump("dbg_xl2", A2v, [t_xl2_done])
        t_dbg_tau = dump("dbg_tau", TA[:], t_tau)

        x1_free = xl2_half[0]
        x2_free = xl2_half[0]
        qw3_free = [[t_av_all, t_dbg_k, t_dbg_v] for _ in range(G3)]
        hid_free = [[t_oproj_done, t_dbg_av] for _ in range(G3)]
        qw12_free = [[t_oproj_done] + list(wpslots[0].free),
                     [t_oproj_done] + list(wpslots[1].free)]
        sg_free = None
        x3ps_free = {id(PS_C): ps_free[id(PS_C)], id(PS_D): ps_free[id(PS_D)]}
        hid_ready_g = []
        q3_toks = []
        t_cc2tok = t_cc2

        def gidx(mat, i):
            return 48 * mat + i

        for i in range(NC_F):
            g, ig = i // G3, i % G3
            qw1 = WQPv[:, 0:2048]
            qw2 = WQPv[:, 2048:4096]
            t_l1 = S.op("sync", lambda e, ii=i: e.dma_start(
                out=qw1, in_=d_qfull[gidx(0, ii)]),
                after=qw12_free[0] + [t_cc2tok], dma="qw1l")
            t_l2 = S.op("sync", lambda e, ii=i: e.dma_start(
                out=qw2, in_=d_qfull[gidx(1, ii)]),
                after=qw12_free[1] + [t_cc2tok], dma="qw2l")
            if ig < 4:
                qw3 = A3v[:, ig * 2048:(ig + 1) * 2048]
            else:
                qw3 = A4v[:, (ig - 4) * 2048:(ig - 4 + 1) * 2048]
            t_l3 = S.op("sync", lambda e, ii=i, q3=qw3: e.dma_start(
                out=q3, in_=d_qfull[gidx(2, ii)]),
                after=list(qw3_free[ig]) + [t_cc2tok], dma=f"q3l{ig}")
            q3_toks.append(t_l3)

            mm1 = None
            for th in range(2):
                for dc in range(16):
                    mm1 = S.op("tensor", lambda e, d=dc, t2=th: e.matmul(
                        PS_A[:, t2 * 512:t2 * 512 + 512],
                        qw1[:, d * 128:(d + 1) * 128],
                        A2v[:, d * 1024 + t2 * 512:d * 1024 + t2 * 512 + 512],
                        start=(d == 0), stop=(d == 15)),
                        after=([t_l1, xl2_half[0], x1_free]
                               if (th == 0 and dc == 0) else
                               ([xl2_half[1]] if (th == 1 and dc == 0)
                                else [])), tok=(th == 1 and dc == 15))
            qw12_free[0] = [mm1]
            mm2 = None
            for th in range(2):
                for dc in range(16):
                    mm2 = S.op("tensor", lambda e, d=dc, t2=th: e.matmul(
                        PS_B[:, t2 * 512:t2 * 512 + 512],
                        qw2[:, d * 128:(d + 1) * 128],
                        A2v[:, d * 1024 + t2 * 512:d * 1024 + t2 * 512 + 512],
                        start=(d == 0), stop=(d == 15)),
                        after=([t_l2, x2_free] if (th == 0 and dc == 0) else
                               ([xl2_half[1]] if (th == 1 and dc == 0)
                                else [])),
                        tok=(th == 1 and dc == 15))
            qw12_free[1] = [mm2]

            hidv = A1v[:, ig * 1024:(ig + 1) * 1024]
            t_hid = None
            tsg = None
            for th in range(2):
                sg = SCv[:, 4096:4608]
                tsg = S.op("scalar", lambda e, t2=th, s2=sg: e.activation(
                    s2, PS_A[:, t2 * 512:t2 * 512 + 512], AF.Sigmoid,
                    scale=0.5), after=[mm1, sg_free], tok=True)
                th1 = S.op("vector", lambda e, t2=th, h=hidv: e.tensor_scalar(
                    h[:, t2 * 512:t2 * 512 + 512],
                    PS_B[:, t2 * 512:t2 * 512 + 512], 0.25, None, OP.mult),
                    after=[mm2] + list(hid_free[ig] or []), tok=True)
                t_hid = S.op("vector", lambda e, t2=th, h=hidv, s2=sg:
                             e.tensor_tensor(h[:, t2 * 512:t2 * 512 + 512],
                                             h[:, t2 * 512:t2 * 512 + 512],
                                             s2, OP.mult),
                             after=[tsg, th1], tok=True)
                sg_free = t_hid
            x1_free = tsg
            x2_free = t_hid
            hid_free[ig] = []
            hid_ready_g.append(t_hid)

            if (ig == G3 - 1) or (i == NC_F - 1):
                n_in_g = ig + 1
                hid_ready = hid_ready_g[-1]
                ld_toks = q3_toks[-n_in_g:]
                t_grp = None
                for dc in range(16):
                    for th in range(2):
                        ps = ps_rot[(dc * 2 + th) % 2]
                        mm = None
                        for j in range(n_in_g):
                            if j < 4:
                                q3 = A3v[:, j * 2048 + dc * 128:
                                         j * 2048 + dc * 128 + 128]
                            else:
                                q3 = A4v[:, (j - 4) * 2048 + dc * 128:
                                         (j - 4) * 2048 + dc * 128 + 128]
                            hj = A1v[:, j * 1024 + th * 512:
                                     j * 1024 + th * 512 + 512]
                            mm = S.op("tensor", lambda e, p=ps, qq=q3, hh_=hj,
                                      j2=j, ng=n_in_g: e.matmul(
                                          p[:], qq, hh_, start=(j2 == 0),
                                          stop=(j2 == ng - 1)),
                                      after=([hid_ready, x3ps_free[id(ps)]]
                                             + ld_toks if j == 0
                                             else []), tok=(j == n_in_g - 1))
                        xsl = Xv[:, dc * 1024 + th * 512:dc * 1024 + th * 512 + 512]
                        ev = S.op("vector", lambda e, p=ps, xs=xsl: e.tensor_tensor(
                            xs, p[:], xs, OP.add),
                            after=[mm, x_write_tok, t_dbg_xattn], tok=True)
                        x3ps_free[id(ps)] = ev
                        t_grp = ev
                x_write_tok = t_grp
                for k in range(G3):
                    qw3_free[k] = [t_grp]
                    hid_free[k] = [t_grp]
                hid_ready_g = []

        t_ffn_done = x_write_tok

        t_out_sem_total = None
        for tt in range(8):
            sidx = tt % 3
            slot = f1slots[sidx]
            evs = []
            slot_guard = list(slot.free)
            slot.free = []
            for r in range(4):
                ps = ps_rot[r % 2]
                trs = None
                for cc in range(4):
                    dc = 4 * r + cc
                    trs = S.op("tensor", lambda e, p=ps, d=dc, c=cc, t2=tt:
                               e.transpose(p[:, c * 128:(c + 1) * 128],
                                           Xv[:, d * 1024 + t2 * 128:
                                              d * 1024 + t2 * 128 + 128], IDF[:]),
                               after=([t_ffn_done, ps_free[id(ps)],
                                       x3ps_free[id(ps)]] + slot_guard
                                      if cc == 0 else []), tok=(cc == 3))
                ev = S.op("vector", lambda e, p=ps, rr=r, s=sidx: e.tensor_copy(
                    F1[s][:, rr * 512:(rr + 1) * 512], p[:, 0:512]),
                    after=[trs] + (slot_guard if r == 0 else []), tok=True)
                ps_free[id(ps)] = ev
                x3ps_free[id(ps)] = ev
                evs.append(ev)
            t_st = S.op("sync", lambda e, s=sidx, t2=tt: e.dma_start(
                out=d_out[t2 * 128:(t2 + 1) * 128, :], in_=F1[s][:]),
                after=evs, dma=f"d_out{sidx}")
            slot.consumed(t_st)

        total_out = [S.cnt.get(f"d_out{i}", 0) for i in range(3)]

        @block.sync
        def _(sync):
            S.emit("sync", sync, sems)

        @block.gpsimd
        def _(gpsimd):
            S.emit("gpsimd", gpsimd, sems)
            for i_ in range(3):
                gpsimd.wait_ge(sems[f"d_out{i_}"], total_out[i_])
            for i_ in range(dump_n[0]):
                gpsimd.wait_ge(sems[f"d_dbg{i_}"], 16)

        @block.tensor
        def _(tensor):
            S.emit("tensor", tensor, sems)

        @block.vector
        def _(vector):
            S.emit("vector", vector, sems)

        @block.scalar
        def _(scalar):
            S.emit("scalar", scalar, sems)

    return nc


def _host_consts():
    ii = np.arange(128)[:, None]
    maskw = np.zeros((128, 640), np.float32)
    uu = np.arange(128)[None, :]
    maskw[:, 0:128] = np.where(uu >= ii, 0.0, NEG)
    maskw[:, 512:640] = np.where(uu <= ii, 0.0, NEG)
    maskw = maskw.astype(ml_dtypes.bfloat16)
    identb = np.eye(128, dtype=ml_dtypes.bfloat16)
    identf = np.eye(128, dtype=np.float32)
    return maskw, identb, identf


def _ng_for(h):
    ng = np.zeros((128, 8), np.float32)
    if h == 0:
        ii = np.arange(128)[:, None]
        for qt in range(8):
            ng[:, qt:qt + 1] = 128.0 * (8 - qt) - ii
    return ng


_NC_CACHE = {}


def _get_nc(debug=False):
    if debug not in _NC_CACHE:
        _NC_CACHE[debug] = build_nc(debug=debug)
    return _NC_CACHE[debug]


def make_in_maps(inputs):
    x = np.asarray(inputs["L0_attn_hidden_states"], np.float32)
    maskw, identb, identf = _host_consts()
    zeros = np.zeros((NT_OWN, H), np.float32)
    common = {
        "wq": np.asarray(inputs["L0_attn_Wq"], np.float32),
        "wk": np.asarray(inputs["L0_attn_Wk"], np.float32),
        "wv": np.asarray(inputs["L0_attn_Wv"], np.float32),
        "wo": np.asarray(inputs["L0_attn_Wo"], np.float32),
        "maskw": maskw, "identb": identb, "identf": identf,
    }
    w1 = np.asarray(inputs["L0_ffn_w1"], np.float32)
    w2 = np.asarray(inputs["L0_ffn_w2"], np.float32)
    w3 = np.asarray(inputs["L0_ffn_w3"], np.float32)
    zunit = np.zeros((128, 2048), np.float32)

    def unit(mat, idx):
        if idx >= 44:
            return zunit
        if mat == 0:
            return w1[idx * 128:(idx + 1) * 128, :]
        if mat == 1:
            return w2[idx * 128:(idx + 1) * 128, :]
        return np.ascontiguousarray(
            w3[:, idx * 128:(idx + 1) * 128].reshape(16, 128, 128)
            .transpose(1, 0, 2).reshape(128, 2048))

    def unit_t(mat, idx):
        u = unit(mat, idx)
        return np.ascontiguousarray(
            u.reshape(128, 16, 128).transpose(2, 1, 0).reshape(128, 2048))

    in_maps = []
    for c in range(8):
        b, h = c >> 1, c & 1
        m = dict(common)
        m["xo"] = np.ascontiguousarray(x[b, h * 1024:(h + 1) * 1024, :])
        m["xa"] = (np.ascontiguousarray(x[b, 0:1024, :]) if h == 1 else zeros)
        m["ng"] = _ng_for(h)
        m["wsl"] = np.stack([unit_t(j // 6, 6 * c + (j % 6))
                             for j in range(18)])
        in_maps.append(m)
    return in_maps


def kernel(**inputs):
    nc = _get_nc(debug=False)
    in_maps = make_in_maps(inputs)
    res = run_bass_kernel_spmd(nc, in_maps, core_ids=list(range(8)))
    out = np.empty((B, S, H), np.float32)
    for c in range(8):
        b, h = c >> 1, c & 1
        out[b, h * 1024:(h + 1) * 1024, :] = res.results[c]["out"]
    return out
